# version 09 (final)
# baseline (speedup 1.0000x reference)
import contextlib

import numpy as np
import ml_dtypes

import concourse.bass as bass
import concourse.mybir as mybir
from concourse.bass_utils import run_bass_kernel_spmd

F32 = mybir.dt.float32
BF16 = mybir.dt.bfloat16
I32 = mybir.dt.int32
U32 = mybir.dt.uint32
AF = mybir.ActivationFunctionType
ALU = mybir.AluOpType
AX = mybir.AxisListType
IOA = bass.IndirectOffsetOnAxis

N_CORES = 8
N = 2048
D = 1024
TT = 16
FC = 8
EPS = 1e-5
SHIFT = 16.0
KVE = 512

E_ROUTED = 7
CAPACITY = 2926
DFF = 4096
LAT = 256
CLAMP_MAX = float(np.log(1.0 / 0.01))
SHARD = (CAPACITY + 127) // 128 * 128
RSN = 8 * SHARD
TRASH = E_ROUTED * SHARD
POOL_COLS = 61440

N_BLOCKS = 4
DEBUG_TAP = None
ATT_PAIRS = None
ATT_STAGE = 99


def k(name, sub=None):
    return (name, sub)


class Prog:
    STREAMS = ("tensor", "vector", "scalar", "gpsimd",
               "sync_q", "scalar_q", "gpsimd_q", "cc")
    ENGINE_OF = {"tensor": "tensor", "vector": "vector", "scalar": "scalar",
                 "gpsimd": "gpsimd", "sync_q": "sync", "scalar_q": "scalar",
                 "gpsimd_q": "gpsimd", "cc": "gpsimd"}
    STEP = {"tensor": 1, "vector": 1, "scalar": 1, "gpsimd": 1,
            "sync_q": 16, "scalar_q": 16, "gpsimd_q": 16, "cc": 1}

    def __init__(self):
        self.items = []
        self.barriers = []

    def mark_barrier(self):
        if not self.barriers or self.barriers[-1] != len(self.items):
            self.barriers.append(len(self.items))

    def emit(self, stream, fn, reads=(), writes=()):
        assert stream in self.STREAMS
        self.items.append((stream, fn, tuple(reads), tuple(writes)))

    def t(self, fn, r=(), w=()):
        self.emit("tensor", fn, r, w)

    def v(self, fn, r=(), w=()):
        self.emit("vector", fn, r, w)

    def s(self, fn, r=(), w=()):
        self.emit("scalar", fn, r, w)

    def g(self, fn, r=(), w=()):
        self.emit("gpsimd", fn, r, w)

    @staticmethod
    def _conf(key, table):
        name, sub = key
        t = table.get(name)
        if not t:
            return []
        out = []
        if sub is None:
            for v in t.values():
                out.extend(v if isinstance(v, list) else [v])
            return out
        for s in (None, sub):
            if s in t:
                v = t[s]
                out.extend(v if isinstance(v, list) else [v])
        return out

    def build(self, nc, block):
        R = 4
        ns = len(self.items)
        last_w, readers = {}, {}
        deps = [set() for _ in range(ns)]
        consumed = [False] * ns
        for i, (stream, fn, reads, writes) in enumerate(self.items):
            for key in reads:
                deps[i].update(self._conf(key, last_w))
            for key in writes:
                deps[i].update(self._conf(key, last_w))
                deps[i].update(self._conf(key, readers))
            deps[i].discard(i)
            for j in deps[i]:
                consumed[j] = True
            for (name, sub) in reads:
                readers.setdefault(name, {}).setdefault(sub, []).append(i)
            for (name, sub) in writes:
                t = last_w.setdefault(name, {})
                rt = readers.setdefault(name, {})
                if sub is None:
                    t.clear()
                    rt.clear()
                else:
                    rt.pop(sub, None)
                t[sub] = i
        sem_of = [None] * ns
        sem_val = [0] * ns
        cnt = {}
        dma_idx = {}
        prev_in_subsem = [None] * ns
        bar_floor = {}
        bar_iter = list(self.barriers)
        bi = 0
        for i, (stream, _, _, _) in enumerate(self.items):
            while bi < len(bar_iter) and bar_iter[bi] == i:
                bar_floor[bar_iter[bi]] = dict(cnt)
                bi += 1
            step = self.STEP[stream]
            if step == 16:
                di = dma_idx.get(stream, 0)
                dma_idx[stream] = di + 1
                sname = f"{stream}{di % R}"
                prev = cnt.get(sname, 0)
                if prev:
                    prev_in_subsem[i] = (sname, prev)
                cnt[sname] = prev + 16
            else:
                sname = stream
                cnt[sname] = cnt.get(sname, 0) + step
            sem_of[i] = sname
            sem_val[i] = cnt[sname]
        while bi < len(bar_iter):
            bar_floor[bar_iter[bi]] = dict(cnt)
            bi += 1
        incs = [self.STEP[s] for (s, _, _, _) in self.items]
        sems = {}
        for sname in cnt:
            sems[sname] = nc.semaphore(f"sem_{sname}").__enter__()
        all_streams = list(cnt.keys())
        know = {}
        pub = {}
        waits = [[] for _ in range(ns)]

        def kw_of(stream):
            if stream not in know:
                know[stream] = {}
            return know[stream]

        bar_applied = {}
        for i, (stream, _, _, _) in enumerate(self.items):
            kw = kw_of(stream)
            need = {}
            for bpos in self.barriers:
                if bpos <= i and bar_applied.get(stream, -1) < bpos:
                    for sname2, v2 in bar_floor[bpos].items():
                        if kw.get(sname2, 0) < v2:
                            need[sname2] = max(need.get(sname2, 0), v2)
                    bar_applied[stream] = bpos
            if prev_in_subsem[i] is not None:
                ps, pv = prev_in_subsem[i]
                if kw.get(ps, 0) < pv:
                    need[ps] = max(need.get(ps, 0), pv)
            for j in deps[i]:
                js, jv = sem_of[j], sem_val[j]
                if kw.get(js, 0) < jv:
                    need[js] = max(need.get(js, 0), jv)
            for js, jv in sorted(need.items()):
                waits[i].append((js, jv))
                kw[js] = max(kw.get(js, 0), jv)
                snap = pub.get((js, jv))
                if snap:
                    for kk, vv in snap.items():
                        if kw.get(kk, 0) < vv:
                            kw[kk] = vv
            if incs[i]:
                snap = dict(kw)
                snap[sem_of[i]] = sem_val[i]
                pub[(sem_of[i], sem_val[i])] = snap
        per_engine = {}
        for i, (stream, _, _, _) in enumerate(self.items):
            per_engine.setdefault(self.ENGINE_OF[stream], []).append(i)
        items = self.items

        def make_body(idxs):
            def body(eng):
                for i in idxs:
                    for (js, jv) in waits[i]:
                        eng.wait_ge(sems[js], jv)
                    inst = items[i][1](eng)
                    if incs[i]:
                        assert inst is not None, f"item {i} returned None"
                        inst.then_inc(sems[sem_of[i]], incs[i])
            return body

        for eng_name, idxs in per_engine.items():
            getattr(block, eng_name)(make_body(idxs))

        @block.sync
        def _(sync):
            for sname, tot in cnt.items():
                if tot:
                    sync.wait_ge(sems[sname], tot)


class Pool:
    def __init__(self, tensor, cols, prog=None):
        self.t = tensor
        self.cols = cols
        self.live = {}
        self.prog = prog

    def alloc(self, name, ncols, dtype=BF16):
        if dtype in (F32, I32, U32):
            ncols *= 2
        ncols = (ncols + 31) & ~31
        taken = sorted(self.live.values())
        off = 0
        for (o, n) in taken:
            if off + ncols <= o:
                break
            off = max(off, o + n)
        assert off + ncols <= self.cols, \
            f"pool overflow: {name} needs {ncols} at {off} (cap {self.cols})"
        self.live[name] = (off, ncols)
        return off

    def free(self, *names):
        for name in names:
            self.live.pop(name)
        if self.prog is not None:
            self.prog.mark_barrier()

    def ap(self, name, dtype=BF16):
        off, ncols = self.live[name]
        a = self.t[:, off:off + ncols]
        if dtype != BF16:
            a = a.bitcast(dtype)
        return a

    def off_bf(self, name):
        return self.live[name][0]


class Cols:
    def __init__(self, tensor, width):
        self.t = tensor
        self.width = width
        self.n = 0
        self.map = {}

    def col(self, name, ncol=1):
        if name not in self.map:
            assert self.n + ncol <= self.width, f"sm overflow at {name}"
            self.map[name] = (self.n, ncol)
            self.n += ncol
        o, nn = self.map[name]
        return self.t[:, o:o + nn]

    def off(self, name):
        return self.map[name][0]


class Ctx:
    pass


def ap3(t, off, pstep, dims):
    return bass.AP(t, off, [[pstep, 128]] + [list(d) for d in dims])


def bcast_row(c, src_row, src_key, dst, dst_key, n):
    P = c.P
    o = 0
    while o < n:
        nn = min(512, n - o)
        P.t(lambda e, o=o, nn=nn: e.matmul(
            c.pW[:, 0:nn], c.onesrow[0:1, :], src_row[0:1, o:o + nn],
            start=True, stop=True),
            [k("onesrow"), src_key, k("pW")], [k("pW")])
        P.v(lambda e, o=o, nn=nn: e.tensor_copy(dst[:, o:o + nn],
                                                c.pW[:, 0:nn]),
            [k("pW")], [dst_key])
        o += nn


def build_program():
    nc = bass.Bass("TRN2", target_bir_lowering=False,
                   dynamic_dma_scratch_size=28672,
                   disable_frame_to_traceback=True)
    c = Ctx()
    c.nc = nc
    c.P = Prog()

    def din(name, shape, dt=F32):
        return nc.dram_tensor(name, shape, dt, kind="ExternalInput").ap()

    c.x_in = din("x", [N, D])
    c.w = {}
    for nm, shape in [
        ("l0_wqT", [D, D]), ("l0_wkT", [D, KVE]), ("l0_wvT", [D, KVE]),
        ("l0_woT", [D, D]), ("l0_W_T", [D, D]), ("l0_WpT", [D, 256]),
        ("l0_WdT", [D, 16]), ("l0_WuT", [16, D]),
        ("l1_Wq", [D, D]), ("l1_Wdkv", [D, LAT]), ("l1_Wuk", [LAT, D]),
        ("l1_Wuv", [LAT, D]), ("l1_Wo", [D, D]), ("l1_Wr", [D, 8]),
        ("myW1_in", [D, DFF]), ("myW3_in", [D, DFF]), ("myW2_in", [DFF, D]),
        ("l1_W1s", [D, DFF]), ("l1_W3s", [D, DFF]), ("l1_W2s", [DFF, D]),
        ("l0_bq_rep", [128, D]), ("l0_bk_rep", [128, KVE]),
        ("l0_bv_rep", [128, KVE]), ("l0_bo_rep", [128, D]),
        ("l0_g_rep", [128, D]), ("l0_b_rep", [128, D]),
        ("l0_bp_rep", [128, 256]), ("l0_sim", [128, 8]), ("l0_temp", [1, 1]),
        ("l1_gamma_pp", [128, 2]), ("l1_beta_pp", [128, 2]),
        ("l1_g_rep", [128, D]), ("l1_b_rep", [128, D]),
    ]:
        c.w[nm] = din(nm, shape)
    c.cst = {}
    for nm, shape, dt in [
        ("idn_bf", [128, 128], BF16), ("idn_f32", [128, 128], F32),
        ("tril", [128, 4 * 512], BF16),
        ("rope_cos", [128, TT * 64], F32), ("rope_sin", [128, TT * 64], F32),
        ("iota8", [128, 8], F32), ("pw8", [128, 8], F32),
        ("tpw8", [128, 8], F32), ("basemat", [128, 64], F32),
        ("trash_iota", [128, 1], F32), ("onescol", [128, 1], F32),
        ("onesrow", [1, 128], F32),
    ]:
        c.cst[nm] = din(nm, shape, dt)

    c.y_out = nc.dram_tensor("y", [N, D], F32, kind="ExternalOutput").ap()
    c.cnt_in = nc.dram_tensor("cnt_in", [64, 1], F32)
    c.cnt_out = nc.dram_tensor("cnt_out", [512, 1], F32, addr_space="Shared")
    c.mslots = nc.dram_tensor("mslots", [RSN, D], BF16)
    c.msred = nc.dram_tensor("msred", [SHARD, D], BF16)
    c.msoutl = nc.dram_tensor("msoutl", [SHARD, D], BF16)
    c.mslotout = nc.dram_tensor("mslotout", [RSN, D], BF16,
                                addr_space="Shared")
    c.tpose = nc.dram_tensor("tpose", [N, D], BF16)
    c.tpose_q = nc.dram_tensor("tpose_q", [N, D], BF16)
    c.tpose_k = nc.dram_tensor("tpose_k", [N, D], BF16)
    c.xspill = nc.dram_tensor("xspill", [N, D], BF16)
    c.dbgf = nc.dram_tensor("dbgf", [128, 1024], F32)
    c.dbgu = nc.dram_tensor("dbgu", [128, 32], U32)

    st = contextlib.ExitStack()

    def sb(name, shape, dt):
        return st.enter_context(nc.sbuf_tensor(name, shape, dt))

    c.x_tm = sb("x_tm", [128, TT * D], BF16)
    c.poolt = sb("poolm", [128, POOL_COLS], BF16)
    c.pool = Pool(c.poolt, POOL_COLS, c.P)
    c.f32a = sb("f32a", [128, 1024], F32)
    c.f32b = sb("f32b", [128, 1024], F32)
    c.i32a = sb("i32a", [128, 1024], I32)
    c.sm = sb("sm", [128, 768], F32)
    c.smc = sb("smc", [128, 64], F32)
    c.cstf = sb("cstf", [128, 2048], F32)
    c.trilb = sb("trilb", [128, 2048], BF16)
    c.cbuf = sb("cbuf", [128, 144], BF16)
    c.logit = sb("logit", [128, 128], F32)
    c.rtr = sb("rtr", [128, 512], F32)
    c.idx32 = sb("idx32", [128, 32], U32)
    c.eA = sb("eA", [128, 1024], BF16)
    c.eB = sb("eB", [128, 1024], BF16)
    c.oe = sb("oe", [128, 1024], F32)
    c.idnf = sb("idnf", [128, 128], F32)
    c.bmat = sb("bmat", [128, 64], F32)
    c.onesrow = sb("onesrow_sb", [1, 128], F32)
    c.zsrc = sb("zsrc", [128, 512], BF16)
    c.cols = Cols(c.sm, 768)

    psum = st.enter_context
    c.pS0 = psum(nc.psum_tensor("pS0", [128, 1024], F32))
    c.pS1 = psum(nc.psum_tensor("pS1", [128, 1024], F32))
    c.pO = psum(nc.psum_tensor("pO", [128, 1024], F32))
    c.pW = psum(nc.psum_tensor("pW", [128, 512], F32))
    c.pbf = psum(nc.psum_tensor("pbf", [128, 512], F32))

    emit_startup(c)
    if N_BLOCKS >= 1:
        emit_l0_attn(c)
    if N_BLOCKS >= 2:
        emit_l0_ffn(c)
    if N_BLOCKS >= 3:
        emit_l1_attn(c)
    if N_BLOCKS >= 4:
        emit_l1_ffn(c)

    P = c.P
    P.emit("gpsimd_q", lambda e: e.dma_start(
        out=bass.AP(c.y_out.tensor, 0, [[1024, 128], [128 * 1024, TT], [1, 1024]]),
        in_=ap3(c.x_tm, 0, TT * D, [(1024, TT), (1, 1024)])),
        [k("x_tm")], [k("y")])

    with st:
        with nc.Block() as block:
            P.build(nc, block)
    return nc


def emit_startup(c):
    P, cst, w, pool = c.P, c.cst, c.w, c.pool
    P.emit("gpsimd_q", lambda e: e.dma_start(
        out=ap3(c.x_tm, 0, TT * D, [(1024, TT), (1, 1024)]),
        in_=bass.AP(c.x_in.tensor, 0, [[1024, 128], [128 * 1024, TT], [1, 1024]])),
        (), [k("x_tm")])
    P.emit("sync_q", lambda e: e.dma_start(out=c.onesrow[:, :],
                                           in_=cst["onesrow"]),
           (), [k("onesrow")])
    if N_BLOCKS >= 4:
        P.v(lambda e: e.memset(c.zsrc[:, :], 0.0), (), [k("zsrc")])
    for nm, dst in [("tril", c.trilb[:, :]), ("rope_cos", c.cstf[:, 0:1024]),
                    ("rope_sin", c.cstf[:, 1024:2048])]:
        P.emit("sync_q", lambda e, nm=nm, dst=dst: e.dma_start(out=dst, in_=cst[nm]),
               (), [k("cstf", nm)])
    P.emit("sync_q", lambda e: e.dma_start(out=c.cbuf[:, 0:128], in_=cst["idn_bf"]),
           (), [k("cbuf", "idn")])
    P.emit("sync_q", lambda e: e.dma_start(out=c.idnf[:, :], in_=cst["idn_f32"]),
           (), [k("idnf")])
    P.emit("sync_q", lambda e: e.dma_start(out=c.bmat[:, :], in_=cst["basemat"]),
           (), [k("bmat")])
    for nm, sl in [("iota8", (0, 8)), ("pw8", (8, 16)), ("tpw8", (16, 24)),
                   ("trash_iota", (24, 25)), ("onescol", (25, 26))]:

        P.emit("sync_q", lambda e, nm=nm, sl=sl: e.dma_start(
            out=c.smc[:, sl[0]:sl[1]], in_=cst[nm]), (), [k("smc", nm)])
    c.pv = {}
    for (nm, o, ncol) in [("l0_sim", 26, 8), ("l1_gamma_pp", 34, 2),
                          ("l1_beta_pp", 36, 2)]:
        c.pv[nm] = c.smc[:, o:o + ncol]
        P.emit("sync_q", lambda e, nm=nm, o=o, ncol=ncol: e.dma_start(
            out=c.smc[:, o:o + ncol], in_=w[nm]), (), [k("smc", nm)])
    P.emit("sync_q", lambda e: e.dma_start(out=c.smc[0:1, 62:63], in_=w["l0_temp"]),
           (), [k("smc", "temp")])
    P.v(lambda e: e.memset(c.sm[:, :], 0.0), (), [k("sm", None)])
    P.v(lambda e: e.memset(c.smc[:, 40:41], -SHIFT), (), [k("smc", "fshift")])
    P.v(lambda e: e.memset(c.smc[:, 41:42], -1.0), (), [k("smc", "fneg1")])
    P.v(lambda e: e.memset(c.smc[:, 42:43], 0.125), (), [k("smc", "fqkscale")])


def load_rep(c, nm, cols):
    P, pool, w = c.P, c.pool, c.w
    pool.alloc("rep_" + nm, cols)
    dst = pool.ap("rep_" + nm)
    P.emit("gpsimd_q", lambda e, nm=nm, dst=dst: e.dma_start(out=dst, in_=w[nm]),
           (), [k("rep", nm)])
    return dst


def ln_stats(c, src_ap, src_keys, tag, rms_fuse):
    P, C = c.P, c.cols
    bn = C.col(tag + "_bn", 12)
    mv = C.col(tag + "_mv", 2)
    P.v(lambda e: e.bn_stats(bn[:, 0:6], src_ap[:, 0:512]),
        src_keys, [k("sm", tag + "bn0")])
    P.v(lambda e: e.bn_stats(bn[:, 6:12], src_ap[:, 512:1024]),
        src_keys, [k("sm", tag + "bn1")])
    P.v(lambda e: e.bn_aggr(mv, bn),
        [k("sm", tag + "bn0"), k("sm", tag + "bn1")], [k("sm", tag + "mv")])
    m = mv[:, 0:1]
    v = mv[:, 1:2]
    a = C.col(tag + "_a")
    r = C.col(tag + "_r")
    P.v(lambda e: e.tensor_scalar(a, v, EPS, None, op0=ALU.add),
        [k("sm", tag + "mv")], [k("sm", tag + "a")])
    P.v(lambda e: e.reciprocal(r, a), [k("sm", tag + "a")], [k("sm", tag + "r")])
    P.s(lambda e: e.activation(r, r, AF.Sqrt), [k("sm", tag + "r")],
        [k("sm", tag + "r")])
    if rms_fuse:
        mz = C.col(tag + "_mz")
        r2 = C.col(tag + "_r2")
        s = C.col(tag + "_s")
        P.v(lambda e: e.tensor_tensor(mz, v, a, op=ALU.divide)
            if False else e.tensor_tensor(mz, v, r, op=ALU.mult),
            [k("sm", tag + "mv"), k("sm", tag + "r")], [k("sm", tag + "mz")])
        P.v(lambda e: e.tensor_tensor(mz, mz, r, op=ALU.mult),
            [k("sm", tag + "mz"), k("sm", tag + "r")], [k("sm", tag + "mz")])
        P.v(lambda e: e.tensor_scalar(mz, mz, EPS, None, op0=ALU.add),
            [k("sm", tag + "mz")], [k("sm", tag + "mz")])
        P.v(lambda e: e.reciprocal(r2, mz), [k("sm", tag + "mz")],
            [k("sm", tag + "r2")])
        P.s(lambda e: e.activation(r2, r2, AF.Sqrt), [k("sm", tag + "r2")],
            [k("sm", tag + "r2")])
        P.v(lambda e: e.tensor_tensor(s, r, r2, op=ALU.mult),
            [k("sm", tag + "r"), k("sm", tag + "r2")], [k("sm", tag + "s")])
        scale, skey = s, k("sm", tag + "s")
    else:
        scale, skey = r, k("sm", tag + "r")
    nb = C.col(tag + "_nb")
    P.v(lambda e: e.tensor_tensor(nb, m, scale, op=ALU.mult),
        [k("sm", tag + "mv"), skey], [k("sm", tag + "nb")])
    P.v(lambda e: e.tensor_scalar(nb, nb, -1.0, None, op0=ALU.mult),
        [k("sm", tag + "nb")], [k("sm", tag + "nb")])
    return scale, skey, nb, k("sm", tag + "nb")


def quant_rows(c, xn_ap, xn_key, dst_ap, dst_key, tag):
    P, C = c.P, c.cols
    rm = C.col(tag + "_rm")
    qs = C.col(tag + "_qs")
    iq = C.col(tag + "_iq")
    P.v(lambda e: e.tensor_reduce(rm, xn_ap, axis=AX.X, op=ALU.max,
                                  apply_absolute_value=True),
        [xn_key], [k("sm", tag + "rm")])
    P.v(lambda e: e.tensor_scalar(rm, rm, 1e-5, None, op0=ALU.max),
        [k("sm", tag + "rm")], [k("sm", tag + "rm")])
    P.v(lambda e: e.reciprocal(qs, rm), [k("sm", tag + "rm")], [k("sm", tag + "qs")])
    P.v(lambda e: e.tensor_scalar(qs, qs, 127.0, None, op0=ALU.mult),
        [k("sm", tag + "qs")], [k("sm", tag + "qs")])
    P.v(lambda e: e.tensor_scalar(iq, rm, 1.0 / 127.0, None, op0=ALU.mult),
        [k("sm", tag + "rm")], [k("sm", tag + "iq")])
    P.v(lambda e: e.tensor_scalar(c.i32a[:, :], xn_ap, qs, None, op0=ALU.mult),
        [xn_key, k("sm", tag + "qs")], [k("i32a")])
    P.v(lambda e: e.tensor_scalar(c.f32b[:, :], c.i32a[:, :], 127.0, -128.0,
                                  op0=ALU.min, op1=ALU.max),
        [k("i32a")], [k("f32b")])
    P.s(lambda e: e.activation(dst_ap, c.f32b[:, :], AF.Identity, scale=iq),
        [k("f32b"), k("sm", tag + "iq")], [dst_key])


def stage_tile(c, src_ap, t, src_key, tag, dram=None, ncol=D):
    P = c.P
    dram = dram if dram is not None else c.tpose
    dst = bass.AP(dram, t * 128 * D, [[D, 128], [1, ncol]])
    P.emit("sync_q", lambda e, dst=dst, src_ap=src_ap: e.dma_start(
        out=dst, in_=src_ap), [src_key], [k(dram.name)])


def fm_chunk(c, dram, ch, dst_ap, src_tag, dst_key, nrows=N):
    P = c.P
    src = bass.AP(dram, ch * 128, [[D, nrows], [1, 128]])
    P.emit("scalar_q", lambda e, src=src, dst_ap=dst_ap: e.dma_start(
        out=dst_ap, in_=src, transpose=True),
        [k(dram.name), k("xbar")], [dst_key, k("xbar")])


def quant_weight(c, nm, src, O):
    P, pool, w, C = c.P, c.pool, c.w, c.cols
    wst = pool.ap("wstage", F32)
    P.emit("scalar_q", lambda e: e.dma_start(
        out=wst[:, 0:FC * O],
        in_=bass.AP(w[src].tensor, 0, [[O, 128], [128 * O, FC], [1, O]])),
        [k("wstage")], [k("wstage")])
    ab = C.col(nm + "_ab", 8)
    for ch in range(FC):
        P.s(lambda e, ch=ch: e.activation(c.f32a[:, 0:O],
                                          wst[:, ch * O:(ch + 1) * O], AF.Abs),
            [k("wstage")], [k("f32a")])
        P.v(lambda e, ch=ch: e.tensor_reduce(ab[:, ch:ch + 1], c.f32a[:, 0:O],
                                             axis=AX.X, op=ALU.add),
            [k("f32a")], [k("sm", nm + "ab")])
    P.t(lambda e: e.matmul(c.pW[0:1, 0:8], c.smc[:, 25:26], ab,
                           start=True, stop=True),
        [k("smc", "onescol"), k("sm", nm + "ab")], [k("pW")])
    tot = C.col(nm + "_tot")
    P.v(lambda e: e.tensor_reduce(tot[0:1, :], c.pW[0:1, 0:8], axis=AX.X,
                                  op=ALU.add), [k("pW")], [k("sm", nm + "t")])
    P.v(lambda e: e.tensor_scalar(tot[0:1, :], tot[0:1, :], 1.0 / (D * O), 1e-5,
                                  op0=ALU.mult, op1=ALU.max),
        [k("sm", nm + "t")], [k("sm", nm + "t")])
    ws = C.col(nm + "_ws")
    P.v(lambda e: e.reciprocal(ws[0:1, :], tot[0:1, :]),
        [k("sm", nm + "t")], [k("sm", nm + "ws")])
    wsb = C.col(nm + "_wsb")
    ivb = C.col(nm + "_ivb")
    bcast_row(c, ws, k("sm", nm + "ws"), wsb, k("sm", nm + "wsb"), 1)
    bcast_row(c, tot, k("sm", nm + "t"), ivb, k("sm", nm + "ivb"), 1)
    dst = pool.ap(nm + "_q")
    for ch in range(FC):
        P.s(lambda e, ch=ch: e.activation(c.f32a[:, 0:O],
                                          wst[:, ch * O:(ch + 1) * O],
                                          AF.Identity, scale=wsb),
            [k("wstage"), k("sm", nm + "wsb")], [k("f32a")])
        P.v(lambda e, ch=ch: e.tensor_scalar(c.i32a[:, 0:O], c.f32a[:, 0:O],
                                             1.0, None, op0=ALU.mult),
            [k("f32a")], [k("i32a")])
        P.v(lambda e, ch=ch, dst=dst: e.tensor_scalar(
            dst[:, ch * O:(ch + 1) * O], c.i32a[:, 0:O], 1.0, -1.0,
            op0=ALU.min, op1=ALU.max),
            [k("i32a")], [k(nm + "_q", ch)])
    return ivb


def proj_tm(c, t, lhs_fm, lhs_name, nkc, w_ap, w_key, psum, psum_name, ng_list,
            evac):
    P = c.P
    for gi, (no, nn) in enumerate(ng_list):
        pb = psum[:, (gi % 2) * 512:(gi % 2) * 512 + nn]
        pkey = k(psum_name, gi % 2)
        for kc in range(nkc):
            P.t(lambda e, t=t, kc=kc, no=no, nn=nn, pb=pb, w_ap=w_ap:
                e.matmul(pb, lhs_fm(kc, t),
                         w_ap(kc, no, nn),
                         start=(kc == 0), stop=(kc == nkc - 1)),
                [w_key(kc), lhs_name(kc), pkey], [pkey])
        evac(gi, no, nn, pb, pkey)


def attention(c, layer, vp_name, vp_stride, nheads, gqa, causal, attn_dst,
              attn_keyname):
    P, pool = c.P, c.pool
    pool.alloc("qbuf", 2 * N)
    pool.alloc("kbuf", 2 * N)
    qb = pool.ap("qbuf")
    kb = pool.ap("kbuf")
    if gqa:
        pool.alloc("k2buf", 2 * N)
        k2b = pool.ap("k2buf")
    vp_off = pool.off_bf(vp_name)
    npair = nheads // 2 if ATT_PAIRS is None else ATT_PAIRS
    for j in range(npair):
        par = j % 2
        fm_chunk(c, c.tpose_q, j, qb[:, par * N:(par + 1) * N], "q",
                 k("qbuf", par))
        kch = j // 2 if gqa else j
        fm_chunk(c, c.tpose_k, kch, kb[:, par * N:(par + 1) * N], "k",
                 k("kbuf", par))
        if gqa:
            for half in range(2):
                P.emit("sync_q", lambda e, j=j, par=par, half=half: e.dma_start(
                    out=k2b[half * 64:half * 64 + 64, par * N:(par + 1) * N],
                    in_=kb[(j % 2) * 64:(j % 2) * 64 + 64,
                           par * N:(par + 1) * N]),
                    [k("kbuf", (par, kch)), k("xbar")],
                    [k("k2buf", (par, half)), k("xbar")])
            kmat = k2b
            kkeys = lambda r, par=par: k("k2buf", (par, r))
        else:
            kmat = kb
            kkeys = lambda r, par=par: k("kbuf", par)
        if ATT_STAGE == 0:
            continue
        for qg in range(4):
            kts = list(range(4 * qg + 4)) if causal else list(range(TT))
            for kt in kts:
                pS = c.pS0 if kt % 2 == 0 else c.pS1
                pSk = k("pS0" if kt % 2 == 0 else "pS1", None)
                for r in range(2):
                    P.t(lambda e, j=j, qg=qg, kt=kt, r=r, pS=pS, par=par,
                        kmat=kmat: e.matmul(
                            pS[:, r * 512:r * 512 + 512],
                            kmat[r * 64:r * 64 + 64,
                                 par * N + kt * 128:par * N + kt * 128 + 128],
                            qb[r * 64:r * 64 + 64,
                               par * N + qg * 512:par * N + qg * 512 + 512],
                            start=True, stop=True, tile_position=(r * 64, 0)),
                        [kkeys(r), k("qbuf", par)], [pSk])
                if causal and kt >= 4 * qg:
                    ri = kt - 4 * qg
                    mask = bass.AP(c.trilb, ri * 512,
                                   [[2048, 128], [0, 2], [1, 512]])
                    P.v(lambda e, pS=pS, mask=mask: e.tensor_tensor(
                        pS[:, :], pS[:, :], mask, op=ALU.add),
                        [pSk, k("cstf", "tril")], [pSk])
                eT = c.eA if kt % 2 == 0 else c.eB
                eTk = k("eA" if kt % 2 == 0 else "eB")
                P.s(lambda e, pS=pS, eT=eT: e.activation(
                    eT[:, :], pS[:, :], AF.Exp, bias=c.smc[:, 40:41],
                    scale=c.smc[:, 42:43]),
                    [pSk, k("smc", "fshift"), k("smc", "fqkscale")], [eTk])
                if ATT_STAGE == 1:
                    continue
                for r in range(2):
                    h = 2 * j + r
                    vh = (h // 2) if gqa else h
                    vsl = bass.AP(c.poolt, vp_off + kt * vp_stride + vh * 66,
                                  [[POOL_COLS, 128], [1, 66]])
                    P.t(lambda e, kt=kt, r=r, eT=eT, vsl=vsl, kts=kts: e.matmul(
                        c.pO[0:66, r * 512:r * 512 + 512], vsl,
                        eT[:, r * 512:r * 512 + 512],
                        start=(kt == kts[0]), stop=(kt == kts[-1])),
                        [k(vp_name, kt), k(vp_name, "ones"), eTk, k("pO")],
                        [k("pO")])
            if ATT_STAGE <= 2:
                continue
            for r in range(2):
                P.v(lambda e, r=r: e.tensor_copy(
                    c.oe[0:66, r * 512:r * 512 + 512],
                    c.pO[0:66, r * 512:r * 512 + 512]),
                    [k("pO")], [k("oe", r)])
            for r in range(2):
                h = 2 * j + r
                for sub in range(4):
                    t = qg * 4 + sub
                    if ATT_STAGE >= 4:
                        P.t(lambda e, r=r, sub=sub: e.transpose(
                            c.pbf[:, sub * 128:sub * 128 + 66],
                            c.oe[0:66,
                                 r * 512 + sub * 128:r * 512 + sub * 128 + 128],
                            c.idnf[0:66, 0:66]),
                            [k("oe", r), k("idnf")], [k("pbf", sub), k("pbfb")])
                    if ATT_STAGE >= 5:
                        zc2 = c.cols.col(f"{layer}z{sub}", 2)
                        zc = zc2[:, 0:1]
                        P.v(lambda e, sub=sub, zc2=zc2: e.tensor_copy(
                            zc2, c.pbf[:, sub * 128 + 64:sub * 128 + 66]),
                            [k("pbf", sub), k("pbfb")],
                            [k("sm", f"{layer}z{sub}")])
                        P.v(lambda e, zc2=zc2: e.reciprocal(zc2, zc2),
                            [k("sm", f"{layer}z{sub}")],
                            [k("sm", f"{layer}z{sub}")])
                    if ATT_STAGE >= 6:
                        P.v(lambda e, sub=sub, t=t, h=h, zc=zc: e.tensor_scalar(
                            attn_dst[:, t * D + h * 64:t * D + h * 64 + 64],
                            c.pbf[:, sub * 128:sub * 128 + 64], zc, None,
                            op0=ALU.mult),
                            [k("pbf", sub), k("pbfb"),
                             k("sm", f"{layer}z{sub}")],
                            [k(attn_keyname, (t, h))])
    pool.free("qbuf", "kbuf")
    if gqa:
        pool.free("k2buf")


def emit_l0_attn(c):
    P, pool, w = c.P, c.pool, c.w

    for nm, cols in [("l0_bq_rep", D), ("l0_bk_rep", KVE), ("l0_bv_rep", KVE),
                     ("l0_g_rep", D), ("l0_b_rep", D)]:
        load_rep(c, nm, cols)

    pool.alloc("wstage", 8192, F32)
    for nm, src, O in [("wq", "l0_wqT", D), ("wk", "l0_wkT", KVE),
                       ("wv", "l0_wvT", KVE), ("wo", "l0_woT", D)]:
        pool.alloc(nm + "_q", FC * O)
    ivq = quant_weight(c, "wq", "l0_wqT", D)
    ivk = quant_weight(c, "wk", "l0_wkT", KVE)
    ivv = quant_weight(c, "wv", "l0_wvT", KVE)
    ivo = quant_weight(c, "wo", "l0_woT", D)
    pool.free("wstage")

    pool.alloc("stg", 1024)
    stg = pool.ap("stg")
    for t in range(TT):
        xt = c.x_tm[:, t * D:(t + 1) * D]
        sc, sck, nb, nbk = ln_stats(c, xt, [k("x_tm")], f"l0a{t % 2}", True)
        P.s(lambda e, xt=xt, sc=sc, nb=nb: e.activation(
            c.f32a[:, :], xt, AF.Identity, scale=sc, bias=nb),
            [k("x_tm"), sck, nbk], [k("f32a")])
        quant_rows(c, c.f32a[:, :], k("f32a"), stg, k("stg"), f"l0q{t % 2}")
        stage_tile(c, stg, t, k("stg"), "xq")
    if DEBUG_TAP == "xq":
        return
    pool.alloc("xq_fm", FC * N)
    xq = pool.ap("xq_fm")
    for ch in range(FC):
        fm_chunk(c, c.tpose, ch, xq[:, ch * N:(ch + 1) * N], "xq", k("xq_fm", ch))

    wqq = pool.ap("wq_q")
    wkq = pool.ap("wk_q")
    wvq = pool.ap("wv_q")
    bqr = pool.ap("rep_l0_bq_rep")
    bkr = pool.ap("rep_l0_bk_rep")
    bvr = pool.ap("rep_l0_bv_rep")
    pool.alloc("Vp", TT * 528)
    vp_off = pool.off_bf("Vp")
    P.v(lambda e: e.memset(bass.AP(c.poolt, vp_off + 64,
                                   [[POOL_COLS, 128], [528, TT], [66, 8], [1, 2]]),
                           1.0), (), [k("Vp", "ones")])

    def lhs_xq(kc, t):
        return xq[:, kc * N + t * 128:kc * N + t * 128 + 128]

    for t in range(TT):
        def evq(gi, no, nn, pb, pkey, t=t):
            P.v(lambda e, pb=pb, no=no, nn=nn: e.scalar_tensor_tensor(
                stg[:, no:no + nn], pb, ivq, bqr[:, no:no + nn],
                op0=ALU.mult, op1=ALU.add),
                [pkey, k("sm", "wqivb"), k("rep", "l0_bq_rep")], [k("stg")])
        proj_tm(c, t, lhs_xq, lambda kc: k("xq_fm", kc), FC,
                lambda kc, no, nn: wqq[:, kc * D + no:kc * D + no + nn],
                lambda kc: k("wq_q", kc), c.pS0, "pS0",
                [(0, 512), (512, 512)], evq)
        stage_tile(c, stg, t, k("stg"), "q", dram=c.tpose_q)
        def evk(gi, no, nn, pb, pkey, t=t):
            P.v(lambda e, pb=pb, no=no, nn=nn: e.scalar_tensor_tensor(
                stg[:, no:no + nn], pb, ivk, bkr[:, no:no + nn],
                op0=ALU.mult, op1=ALU.add),
                [pkey, k("sm", "wkivb"), k("rep", "l0_bk_rep")], [k("stg")])
        proj_tm(c, t, lhs_xq, lambda kc: k("xq_fm", kc), FC,
                lambda kc, no, nn: wkq[:, kc * KVE + no:kc * KVE + no + nn],
                lambda kc: k("wk_q", kc), c.pS0, "pS0", [(0, 512)], evk)
        stage_tile(c, stg[:, 0:KVE], t, k("stg"), "k", dram=c.tpose_k, ncol=KVE)
        def evv(gi, no, nn, pb, pkey, t=t):
            vdst = bass.AP(c.poolt, vp_off + t * 528,
                           [[POOL_COLS, 128], [66, 8], [1, 64]])
            P.v(lambda e, pb=pb, vdst=vdst: e.scalar_tensor_tensor(
                vdst, pb, ivv, bvr[:, 0:KVE], op0=ALU.mult, op1=ALU.add),
                [pkey, k("sm", "wvivb"), k("rep", "l0_bv_rep")], [k("Vp", t)])
        proj_tm(c, t, lhs_xq, lambda kc: k("xq_fm", kc), FC,
                lambda kc, no, nn: wvq[:, kc * KVE + no:kc * KVE + no + nn],
                lambda kc: k("wv_q", kc), c.pS1, "pS1", [(0, 512)], evv)
    pool.free("xq_fm", "wq_q", "wk_q", "wv_q", "stg",
              "rep_l0_bq_rep", "rep_l0_bk_rep", "rep_l0_bv_rep")
    if DEBUG_TAP == "qk":
        return

    pool.alloc("attn_tm", TT * D)
    attn = pool.ap("attn_tm")
    attention(c, "l0", "Vp", 528, 16, True, False, attn, "attn")
    pool.free("Vp")
    if DEBUG_TAP == "attn":
        return

    load_rep(c, "l0_bo_rep", D)
    bor = pool.ap("rep_l0_bo_rep")
    gr = pool.ap("rep_l0_g_rep")
    br = pool.ap("rep_l0_b_rep")
    pool.alloc("stg2", 1024)
    stg2 = pool.ap("stg2")
    for t in range(TT):
        at = attn[:, t * D:(t + 1) * D]
        sc, sck, nb, nbk = ln_stats(c, at, [k("attn", None)], f"l0o{t % 2}", False)
        P.s(lambda e, at=at, sc=sc, nb=nb: e.activation(
            c.f32a[:, :], at, AF.Identity, scale=sc, bias=nb),
            [k("attn", None), sck, nbk], [k("f32a")])
        P.v(lambda e: e.tensor_tensor(c.f32a[:, :], c.f32a[:, :], gr,
                                      op=ALU.mult),
            [k("f32a"), k("rep", "l0_g_rep")], [k("f32a")])
        P.v(lambda e: e.tensor_tensor(c.f32a[:, :], c.f32a[:, :], br,
                                      op=ALU.add),
            [k("f32a"), k("rep", "l0_b_rep")], [k("f32a")])
        ssq = c.cols.col(f"l0y{t % 2}_ss")
        P.s(lambda e, ssq=ssq: e.activation(c.f32b[:, :], c.f32a[:, :],
                                            AF.Square, accum_out=ssq),
            [k("f32a")], [k("f32b"), k("sm", f"l0yss{t % 2}")])
        P.v(lambda e, ssq=ssq: e.tensor_scalar(ssq, ssq, 1.0 / D, EPS,
                                               op0=ALU.mult, op1=ALU.add),
            [k("sm", f"l0yss{t % 2}")], [k("sm", f"l0yss{t % 2}")])
        P.v(lambda e, ssq=ssq: e.reciprocal(ssq, ssq),
            [k("sm", f"l0yss{t % 2}")], [k("sm", f"l0yss{t % 2}")])
        P.s(lambda e, ssq=ssq: e.activation(ssq, ssq, AF.Sqrt),
            [k("sm", f"l0yss{t % 2}")], [k("sm", f"l0yss{t % 2}")])
        P.s(lambda e, ssq=ssq: e.activation(c.f32a[:, :], c.f32a[:, :],
                                            AF.Identity, scale=ssq),
            [k("f32a"), k("sm", f"l0yss{t % 2}")], [k("f32a")])
        quant_rows(c, c.f32a[:, :], k("f32a"), stg2, k("stg2"), f"l0r{t % 2}")
        stage_tile(c, stg2, t, k("stg2"), "x2")
    pool.free("attn_tm", "stg2", "rep_l0_g_rep", "rep_l0_b_rep")
    pool.alloc("x2_fm", FC * N)
    x2 = pool.ap("x2_fm")
    for ch in range(FC):
        fm_chunk(c, c.tpose, ch, x2[:, ch * N:(ch + 1) * N], "x2", k("x2_fm", ch))
    woq = pool.ap("wo_q")
    for t in range(TT):
        def evo(gi, no, nn, pb, pkey, t=t):
            P.v(lambda e, pb=pb, no=no, nn=nn: e.scalar_tensor_tensor(
                c.f32a[:, no:no + nn], pb, ivo, bor[:, no:no + nn],
                op0=ALU.mult, op1=ALU.add),
                [pkey, k("sm", "woivb"), k("rep", "l0_bo_rep")], [k("f32a")])
            P.v(lambda e, t=t, no=no, nn=nn: e.tensor_tensor(
                c.x_tm[:, t * D + no:t * D + no + nn],
                c.x_tm[:, t * D + no:t * D + no + nn],
                c.f32a[:, no:no + nn], op=ALU.add),
                [k("f32a"), k("x_tm")], [k("x_tm")])
        proj_tm(c, t, lambda kc, t2: x2[:, kc * N + t2 * 128:kc * N + t2 * 128 + 128],
                lambda kc: k("x2_fm", kc), FC,
                lambda kc, no, nn: woq[:, kc * D + no:kc * D + no + nn],
                lambda kc: k("wo_q", kc), c.pS0, "pS0",
                [(0, 512), (512, 512)], evo)
    pool.free("x2_fm", "wo_q", "rep_l0_bo_rep")


def emit_l0_ffn(c):
    P, pool, w = c.P, c.pool, c.w
    load_rep(c, "l0_bp_rep", 256)
    bpr = pool.ap("rep_l0_bp_rep")
    pool.alloc("W_bf", FC * D)
    pool.alloc("Wp_bf", FC * 256)
    pool.alloc("Wd_bf", FC * 16)
    pool.alloc("Wu_bf", D)
    Wb = pool.ap("W_bf")
    Wpb = pool.ap("Wp_bf")
    Wdb = pool.ap("Wd_bf")
    Wub = pool.ap("Wu_bf")
    for nm, src, cols, dst in [("W_bf", "l0_W_T", D, Wb),
                               ("Wp_bf", "l0_WpT", 256, Wpb),
                               ("Wd_bf", "l0_WdT", 16, Wdb)]:
        P.emit("gpsimd_q", lambda e, src=src, cols=cols, dst=dst: e.dma_start(
            out=dst, in_=bass.AP(w[src].tensor, 0,
                                 [[cols, 128], [128 * cols, FC], [1, cols]])),
            (), [k(nm)])
    P.emit("gpsimd_q", lambda e: e.dma_start(out=Wub[0:16, :], in_=w["l0_WuT"]),
           (), [k("Wu_bf")])
    if N_BLOCKS >= 4:
        for zi in range(23):
            P.emit("gpsimd_q", lambda e, zi=zi: e.dma_start(
                out=bass.AP(c.mslots, zi * 8 * 128 * D,
                            [[8192, 128], [512, 16], [1, 512]]),
                in_=bass.AP(c.zsrc, 0, [[512, 128], [0, 16], [1, 512]])),
                [k("zsrc")], [k("mslots", ("z", zi))])

    pool.alloc("stg", 1024)
    stg = pool.ap("stg")
    for t in range(TT):
        xt = c.x_tm[:, t * D:(t + 1) * D]
        sc, sck, nb, nbk = ln_stats(c, xt, [k("x_tm")], f"f0{t % 2}", False)
        P.s(lambda e, xt=xt, sc=sc, nb=nb: e.activation(
            stg, xt, AF.Identity, scale=sc, bias=nb),
            [k("x_tm"), sck, nbk], [k("stg")])
        stage_tile(c, stg, t, k("stg"), "h0")
    pool.alloc("h_fm", FC * N)
    hf = pool.ap("h_fm")
    for ch in range(FC):
        fm_chunk(c, c.tpose, ch, hf[:, ch * N:(ch + 1) * N], "h0", k("h_fm", ch))

    def lhs_h(kc, t):
        return hf[:, kc * N + t * 128:kc * N + t * 128 + 128]

    for t in range(TT):
        def evb(gi, no, nn, pb, pkey, t=t):
            P.v(lambda e, pb=pb, t=t, no=no, nn=nn: e.tensor_tensor(
                c.x_tm[:, t * D + no:t * D + no + nn],
                c.x_tm[:, t * D + no:t * D + no + nn], pb, op=ALU.add),
                [pkey, k("x_tm")], [k("x_tm")])
        proj_tm(c, t, lhs_h, lambda kc: k("h_fm", kc), FC,
                lambda kc, no, nn: Wb[:, kc * D + no:kc * D + no + nn],
                lambda kc: k("W_bf"), c.pS0, "pS0", [(0, 512), (512, 512)], evb)

    sim = c.pv["l0_sim"]
    simnb = c.cbuf[:, 128:136]
    P.s(lambda e: e.activation(c.f32a[:, 0:8], sim, AF.Square),
        [k("smc", "l0_sim")], [k("f32a")])
    P.t(lambda e: e.matmul(c.pW[0:1, 0:8], c.smc[:, 25:26], c.f32a[:, 0:8],
                           start=True, stop=True),
        [k("smc", "onescol"), k("f32a")], [k("pW")])
    cn = c.cols.col("simc", 4)
    cn8 = c.cols.col("simc8", 8)
    P.v(lambda e: e.tensor_copy(cn8[0:1, :], c.pW[0:1, 0:8]),
        [k("pW")], [k("sm", "simc8")])
    P.v(lambda e: e.tensor_tensor(cn[0:1, :], cn8[0:1, 0:4], cn8[0:1, 4:8],
                                  op=ALU.add), [k("sm", "simc8")], [k("sm", "simc")])
    P.s(lambda e: e.activation(cn[0:1, :], cn[0:1, :], AF.Sqrt),
        [k("sm", "simc")], [k("sm", "simc")])
    P.v(lambda e: e.tensor_scalar(cn[0:1, :], cn[0:1, :], 1e-12, None,
                                  op0=ALU.max),
        [k("sm", "simc")], [k("sm", "simc")])
    P.v(lambda e: e.reciprocal(cn[0:1, :], cn[0:1, :]),
        [k("sm", "simc")], [k("sm", "simc")])
    te = c.cols.col("temp")
    P.v(lambda e: e.tensor_scalar(te[0:1, :], c.smc[0:1, 62:63], CLAMP_MAX, None,
                                  op0=ALU.min), [k("smc", "temp")], [k("sm", "te")])
    P.s(lambda e: e.activation(te[0:1, :], te[0:1, :], AF.Exp),
        [k("sm", "te")], [k("sm", "te")])
    P.v(lambda e: e.tensor_scalar(cn[0:1, :], cn[0:1, :], te[0:1, 0:1], None,
                                  op0=ALU.mult),
        [k("sm", "simc"), k("sm", "te")], [k("sm", "simc")])
    cnb = c.cols.col("simcb", 4)
    bcast_row(c, cn, k("sm", "simc"), cnb, k("sm", "simcb"), 4)
    P.v(lambda e: e.tensor_tensor(simnb[:, 0:4], sim[:, 0:4], cnb, op=ALU.mult),
        [k("smc", "l0_sim"), k("sm", "simcb")], [k("cbuf", "simn")])
    P.v(lambda e: e.tensor_tensor(simnb[:, 4:8], sim[:, 4:8], cnb, op=ALU.mult),
        [k("smc", "l0_sim"), k("sm", "simcb")], [k("cbuf", "simn")])

    pool.alloc("stgp", 256)
    stgp = pool.ap("stgp")
    for t in range(TT):
        pb = c.pS1[:, 0:256]
        pk = k("pS1", None)
        for kc in range(FC):
            P.t(lambda e, t=t, kc=kc, pb=pb: e.matmul(
                pb, lhs_h(kc, t), Wpb[:, kc * 256:(kc + 1) * 256],
                start=(kc == 0), stop=(kc == FC - 1)),
                [k("Wp_bf"), k("h_fm", kc), pk], [pk])
        P.v(lambda e, pb=pb: e.tensor_tensor(c.f32a[:, 0:256], pb, bpr,
                                             op=ALU.add),
            [pk, k("rep", "l0_bp_rep")], [k("f32a")])
        ss = c.cols.col(f"pj{t % 2}_ss")
        P.s(lambda e, ss=ss: e.activation(c.f32b[:, 0:256], c.f32a[:, 0:256],
                                          AF.Square, accum_out=ss),
            [k("f32a")], [k("f32b"), k("sm", f"pjss{t % 2}")])
        P.s(lambda e, ss=ss: e.activation(ss, ss, AF.Sqrt),
            [k("sm", f"pjss{t % 2}")], [k("sm", f"pjss{t % 2}")])
        P.v(lambda e, ss=ss: e.tensor_scalar(ss, ss, 1e-12, None, op0=ALU.max),
            [k("sm", f"pjss{t % 2}")], [k("sm", f"pjss{t % 2}")])
        P.v(lambda e, ss=ss: e.reciprocal(ss, ss),
            [k("sm", f"pjss{t % 2}")], [k("sm", f"pjss{t % 2}")])
        P.s(lambda e, ss=ss: e.activation(stgp, c.f32a[:, 0:256], AF.Identity,
                                          scale=ss),
            [k("f32a"), k("sm", f"pjss{t % 2}")], [k("stgp")])
        stage_tile(c, stgp, t, k("stgp"), "pj", ncol=256)
    pool.alloc("pj_fm", 2 * N)
    pj = pool.ap("pj_fm")
    for ch in range(2):
        fm_chunk(c, c.tpose, ch, pj[:, ch * N:(ch + 1) * N], "pj", k("pj_fm", ch))
    for t in range(TT):
        pb = c.pW[:, 0:4]
        pk = k("pW")
        for kc in range(2):
            P.t(lambda e, t=t, kc=kc, pb=pb: e.matmul(
                pb, pj[:, kc * N + t * 128:kc * N + t * 128 + 128],
                simnb[:, kc * 4:kc * 4 + 4],
                start=(kc == 0), stop=(kc == 1)),
                [k("pj_fm", kc), k("cbuf", "simn"), pk], [pk])
        P.v(lambda e, t=t, pb=pb: e.tensor_copy(c.logit[:, t * 4:t * 4 + 4], pb),
            [pk], [k("logit", t)])
    l3 = bass.AP(c.logit, 0, [[128, 128], [4, TT], [1, 4]])
    lkeys = [k("logit", t) for t in range(TT)]
    mx = c.cols.col("ffmx", 16)
    P.v(lambda e: e.tensor_reduce(mx, l3, axis=AX.X, op=ALU.max, negate=True),
        lkeys, [k("sm", "ffmx")])
    for t in range(TT):
        zz = c.cols.col(f"ffz{t % 2}")
        pmx = c.cols.col(f"ffpm{t % 2}")
        P.s(lambda e, t=t, zz=zz: e.activation(
            c.f32a[:, 0:4], c.logit[:, t * 4:t * 4 + 4], AF.Exp,
            bias=mx[:, t:t + 1], accum_out=zz),
            [k("logit", t), k("sm", "ffmx")], [k("f32a"), k("sm", f"ffz{t % 2}")])
        P.v(lambda e, zz=zz: e.reciprocal(zz, zz),
            [k("sm", f"ffz{t % 2}")], [k("sm", f"ffz{t % 2}")])
        P.v(lambda e, t=t, pmx=pmx: e.tensor_scalar(
            pmx, mx[:, t:t + 1], -1.0, None, op0=ALU.mult),
            [k("sm", "ffmx")], [k("sm", f"ffpm{t % 2}")])
        P.v(lambda e, t=t, pmx=pmx: e.tensor_scalar(
            c.rtr[:, t * 4:t * 4 + 4], c.logit[:, t * 4:t * 4 + 4], pmx, None,
            op0=ALU.is_ge),
            [k("logit", t), k("sm", f"ffpm{t % 2}")], [k("rtr", "ffm")])
        P.v(lambda e, t=t, zz=zz: e.tensor_scalar(
            c.rtr[:, t * 4:t * 4 + 4], c.rtr[:, t * 4:t * 4 + 4], zz, None,
            op0=ALU.mult),
            [k("rtr", "ffm"), k("sm", f"ffz{t % 2}")], [k("rtr", "ffm")])

    pool.alloc("dw_tm", TT * 16, F32)
    dw = pool.ap("dw_tm", F32)
    offs = [(0, 1), (1, 2), (3, 4), (7, 8)]
    for t in range(TT):
        pb = c.pW[:, 0:16]
        pk = k("pW")
        for kc in range(FC):
            P.t(lambda e, t=t, kc=kc, pb=pb: e.matmul(
                pb, lhs_h(kc, t), Wdb[:, kc * 16:kc * 16 + 16],
                start=(kc == 0), stop=(kc == FC - 1)),
                [k("Wd_bf"), k("h_fm", kc), pk], [pk])
        for ji, (o, r) in enumerate(offs):
            P.v(lambda e, t=t, ji=ji, o=o, r=r, pb=pb: e.tensor_scalar(
                dw[:, t * 16 + o:t * 16 + o + r], pb[:, o:o + r],
                c.rtr[:, t * 4 + ji:t * 4 + ji + 1], None, op0=ALU.mult),
                [pk, k("rtr", "ffm")], [k("dw_tm", t)])
        P.v(lambda e, t=t: e.memset(dw[:, t * 16 + 15:t * 16 + 16], 0.0),
            (), [k("dw_tm", t)])
    pool.alloc("dw_fm", N)
    dwf = pool.ap("dw_fm")
    for t in range(TT):
        P.t(lambda e, t=t: e.transpose(c.pbf[0:16, 0:128],
                                       dw[:, t * 16:t * 16 + 16],
                                       c.idnf[:, :]),
            [k("dw_tm", t), k("idnf")], [k("pbf")])
        P.v(lambda e, t=t: e.tensor_copy(dwf[0:16, t * 128:t * 128 + 128],
                                         c.pbf[0:16, 0:128]),
            [k("pbf")], [k("dw_fm", t)])
    for t in range(TT):
        for ng in range(2):
            pb = c.pS0[:, (ng % 2) * 512:(ng % 2) * 512 + 512]
            pk = k("pS0", ng % 2)
            P.t(lambda e, t=t, ng=ng, pb=pb: e.matmul(
                pb, dwf[0:16, t * 128:t * 128 + 128],
                Wub[0:16, ng * 512:ng * 512 + 512], start=True, stop=True),
                [k("dw_fm", t), k("Wu_bf")], [pk])
            P.v(lambda e, t=t, ng=ng, pb=pb: e.tensor_tensor(
                c.x_tm[:, t * D + ng * 512:t * D + ng * 512 + 512],
                c.x_tm[:, t * D + ng * 512:t * D + ng * 512 + 512],
                pb, op=ALU.add),
                [pk, k("x_tm")], [k("x_tm")])
    pool.free("W_bf", "Wp_bf", "Wd_bf", "Wu_bf", "stg", "stgp", "pj_fm",
              "dw_tm", "dw_fm", "h_fm", "rep_l0_bp_rep")


def emit_l1_attn(c):
    P, pool, w = c.P, c.pool, c.w
    for nm, src, cols in [("Wq_bf", "l1_Wq", D), ("Wdkv_bf", "l1_Wdkv", LAT)]:
        pool.alloc(nm, FC * cols)
        dst = pool.ap(nm)
        P.emit("gpsimd_q", lambda e, src=src, cols=cols, dst=dst: e.dma_start(
            out=dst, in_=bass.AP(w[src].tensor, 0,
                                 [[cols, 128], [128 * cols, FC], [1, cols]])),
            (), [k(nm)])
    for nm, src in [("Wuk_bf", "l1_Wuk"), ("Wuv_bf", "l1_Wuv")]:
        pool.alloc(nm, 2 * D)
        dst = pool.ap(nm)
        P.emit("gpsimd_q", lambda e, src=src, dst=dst: e.dma_start(
            out=dst, in_=bass.AP(w[src].tensor, 0,
                                 [[D, 128], [128 * D, 2], [1, D]])),
            (), [k(nm, "raw")])
    Wqb = pool.ap("Wq_bf")
    Wdkvb = pool.ap("Wdkv_bf")
    Wukb = pool.ap("Wuk_bf")
    Wuvb = pool.ap("Wuv_bf")
    gam = c.pv["l1_gamma_pp"]
    bet = c.pv["l1_beta_pp"]
    for nm, buf in [("Wuk_bf", Wukb), ("Wuv_bf", Wuvb)]:
        for ch in range(2):
            P.v(lambda e, buf=buf, ch=ch, nm=nm: e.tensor_scalar(
                buf[:, ch * D:(ch + 1) * D], buf[:, ch * D:(ch + 1) * D],
                gam[:, ch:ch + 1], None, op0=ALU.mult),
                [k(nm, "raw"), k("smc", "l1_gamma_pp")], [k(nm, "g")])
    c.bkrep = {}
    betb = c.cbuf[:, 136:138]
    P.v(lambda e: e.tensor_copy(betb, bet), [k("smc", "l1_beta_pp")],
        [k("cbuf", "betb")])
    for nm, buf in [("Wuk_bf", Wukb), ("Wuv_bf", Wuvb)]:
        pool.alloc(nm + "_brow", D, F32)
        brow = pool.ap(nm + "_brow", F32)
        for ng in range(2):
            pk = k("pW")
            for ch in range(2):
                P.t(lambda e, buf=buf, ch=ch, ng=ng: e.matmul(
                    c.pW[0:1, 0:512], betb[:, ch:ch + 1],
                    buf[:, ch * D + ng * 512:ch * D + ng * 512 + 512],
                    start=(ch == 0), stop=(ch == 1)),
                    [k(nm, "g"), k("cbuf", "betb"), pk], [pk])
            P.v(lambda e, brow=brow, ng=ng: e.tensor_copy(
                brow[0:1, ng * 512:ng * 512 + 512], c.pW[0:1, 0:512]),
                [pk], [k(nm + "_brow", "r")])
        bcast_row(c, brow, k(nm + "_brow", "r"), brow, k(nm + "_brow"), D)
        c.bkrep[nm] = brow

    pool.alloc("stg", 1024)
    stg = pool.ap("stg")
    for t in range(TT):
        xt = c.x_tm[:, t * D:(t + 1) * D]
        sc, sck, nb, nbk = ln_stats(c, xt, [k("x_tm")], f"a1{t % 2}", False)
        P.s(lambda e, xt=xt, sc=sc, nb=nb: e.activation(
            stg, xt, AF.Identity, scale=sc, bias=nb),
            [k("x_tm"), sck, nbk], [k("stg")])
        stage_tile(c, stg, t, k("stg"), "xn")
    pool.alloc("xn_fm", FC * N)
    xn = pool.ap("xn_fm")
    for ch in range(FC):
        fm_chunk(c, c.tpose, ch, xn[:, ch * N:(ch + 1) * N], "xn", k("xn_fm", ch))

    def lhs_xn(kc, t):
        return xn[:, kc * N + t * 128:kc * N + t * 128 + 128]

    pool.alloc("latst", 256)
    lstg = pool.ap("latst")
    for t in range(TT):
        pb = c.pS1[:, 0:256]
        pk = k("pS1", None)
        for kc in range(FC):
            P.t(lambda e, t=t, kc=kc, pb=pb: e.matmul(
                pb, lhs_xn(kc, t), Wdkvb[:, kc * LAT:(kc + 1) * LAT],
                start=(kc == 0), stop=(kc == FC - 1)),
                [k("Wdkv_bf"), k("xn_fm", kc), pk], [pk])
        bn = c.cols.col(f"lat{t % 2}_bn", 6)
        mv = c.cols.col(f"lat{t % 2}_mv", 2)
        P.v(lambda e, pb=pb, bn=bn: e.bn_stats(bn, pb),
            [pk], [k("sm", f"latbn{t % 2}")])
        P.v(lambda e, bn=bn, mv=mv: e.bn_aggr(mv, bn),
            [k("sm", f"latbn{t % 2}")], [k("sm", f"latmv{t % 2}")])
        a = c.cols.col(f"lat{t % 2}_a")
        nb2 = c.cols.col(f"lat{t % 2}_nb")
        P.v(lambda e, a=a, mv=mv: e.tensor_scalar(a, mv[:, 1:2], EPS, None,
                                                  op0=ALU.add),
            [k("sm", f"latmv{t % 2}")], [k("sm", f"lata{t % 2}")])
        P.v(lambda e, a=a: e.reciprocal(a, a),
            [k("sm", f"lata{t % 2}")], [k("sm", f"lata{t % 2}")])
        P.s(lambda e, a=a: e.activation(a, a, AF.Sqrt),
            [k("sm", f"lata{t % 2}")], [k("sm", f"lata{t % 2}")])
        P.v(lambda e, nb2=nb2, mv=mv, a=a: e.tensor_tensor(
            nb2, mv[:, 0:1], a, op=ALU.mult),
            [k("sm", f"latmv{t % 2}"), k("sm", f"lata{t % 2}")],
            [k("sm", f"latnb{t % 2}")])
        P.v(lambda e, nb2=nb2: e.tensor_scalar(nb2, nb2, -1.0, None,
                                               op0=ALU.mult),
            [k("sm", f"latnb{t % 2}")], [k("sm", f"latnb{t % 2}")])
        P.s(lambda e, pb=pb, a=a, nb2=nb2: e.activation(
            lstg, pb, AF.Identity, scale=a, bias=nb2),
            [pk, k("sm", f"lata{t % 2}"), k("sm", f"latnb{t % 2}")],
            [k("latst")])
        stage_tile(c, lstg, t, k("latst"), "lat", ncol=LAT)
    pool.alloc("latn_fm", 2 * N)
    latn = pool.ap("latn_fm")
    for ch in range(2):
        fm_chunk(c, c.tpose, ch, latn[:, ch * N:(ch + 1) * N], "lat",
                 k("latn_fm", ch))

    pool.alloc("Vp", TT * 1056)
    vp_off = pool.off_bf("Vp")
    P.v(lambda e: e.memset(bass.AP(c.poolt, vp_off + 64,
                                   [[POOL_COLS, 128], [1056, TT], [66, 16],
                                    [1, 2]]), 1.0), (), [k("Vp", "ones")])

    def rope_apply(t, dst_ap, dst_key):
        cos_ap = bass.AP(c.cstf, t * 64, [[2048, 128], [0, 16], [1, 64]])
        sin_ap = bass.AP(c.cstf, 1024 + t * 64, [[2048, 128], [0, 16], [1, 64]])
        swap = bass.AP(c.f32a, 32, [[1024, 128], [64, 16], [-32, 2], [1, 32]])
        P.v(lambda e: e.tensor_tensor(c.f32b[:, :], swap, sin_ap, op=ALU.mult),
            [k("f32a"), k("cstf", "rope_sin")], [k("f32b")])
        P.v(lambda e: e.tensor_tensor(c.f32a[:, :], c.f32a[:, :], cos_ap,
                                      op=ALU.mult),
            [k("f32a"), k("cstf", "rope_cos")], [k("f32a")])
        P.v(lambda e: e.tensor_tensor(dst_ap, c.f32a[:, :], c.f32b[:, :],
                                      op=ALU.add),
            [k("f32a"), k("f32b")], [dst_key])

    for t in range(TT):
        for ng in range(2):
            pb = c.pS0[:, ng * 512:ng * 512 + 512]
            pk = k("pS0", ng)
            for kc in range(FC):
                P.t(lambda e, t=t, ng=ng, kc=kc, pb=pb: e.matmul(
                    pb, lhs_xn(kc, t),
                    Wqb[:, kc * D + ng * 512:kc * D + ng * 512 + 512],
                    start=(kc == 0), stop=(kc == FC - 1)),
                    [k("Wq_bf"), k("xn_fm", kc), pk], [pk])
            P.v(lambda e, ng=ng, pb=pb: e.tensor_copy(
                c.f32a[:, ng * 512:ng * 512 + 512], pb), [pk], [k("f32a")])
        rope_apply(t, stg, k("stg"))
        stage_tile(c, stg, t, k("stg"), "q", dram=c.tpose_q)
        for ng in range(2):
            pb = c.pS0[:, ng * 512:ng * 512 + 512]
            pk = k("pS0", ng)
            for kc in range(2):
                P.t(lambda e, t=t, ng=ng, kc=kc, pb=pb: e.matmul(
                    pb, latn[:, kc * N + t * 128:kc * N + t * 128 + 128],
                    Wukb[:, kc * D + ng * 512:kc * D + ng * 512 + 512],
                    start=(kc == 0), stop=(kc == 1)),
                    [k("Wuk_bf", "g"), k("latn_fm", kc), pk], [pk])
            P.v(lambda e, ng=ng, pb=pb: e.tensor_tensor(
                c.f32a[:, ng * 512:ng * 512 + 512], pb,
                c.bkrep["Wuk_bf"][:, ng * 512:ng * 512 + 512], op=ALU.add),
                [pk, k("Wuk_bf_brow")], [k("f32a")])
        rope_apply(t, stg, k("stg"))
        stage_tile(c, stg, t, k("stg"), "k", dram=c.tpose_k)
        for ng in range(2):
            pb = c.pS1[:, ng * 512:ng * 512 + 512]
            pk = k("pS1", ng)
            for kc in range(2):
                P.t(lambda e, t=t, ng=ng, kc=kc, pb=pb: e.matmul(
                    pb, latn[:, kc * N + t * 128:kc * N + t * 128 + 128],
                    Wuvb[:, kc * D + ng * 512:kc * D + ng * 512 + 512],
                    start=(kc == 0), stop=(kc == 1)),
                    [k("Wuv_bf", "g"), k("latn_fm", kc), pk], [pk])
            vdst = bass.AP(c.poolt, vp_off + t * 1056 + ng * 8 * 66,
                           [[POOL_COLS, 128], [66, 8], [1, 64]])
            P.v(lambda e, ng=ng, pb=pb, vdst=vdst: e.tensor_tensor(
                vdst, pb, c.bkrep["Wuv_bf"][:, ng * 512:ng * 512 + 512],
                op=ALU.add),
                [pk, k("Wuv_bf_brow")], [k("Vp", t)])
    pool.free("latn_fm", "latst", "Wdkv_bf", "Wuk_bf", "Wuv_bf",
              "Wuk_bf_brow", "Wuv_bf_brow", "xn_fm", "stg", "Wq_bf")

    pool.alloc("attn_tm", TT * D)
    attn = pool.ap("attn_tm")
    attention(c, "l1", "Vp", 1056, 16, False, True, attn, "attn1")
    pool.free("Vp")
    for t in range(TT):
        stage_tile(c, attn[:, t * D:(t + 1) * D], t, k("attn1", None), "ao")
    pool.free("attn_tm")
    pool.alloc("Wo_bf", FC * D)
    Wob = pool.ap("Wo_bf")
    P.emit("gpsimd_q", lambda e: e.dma_start(
        out=Wob, in_=bass.AP(w["l1_Wo"].tensor, 0,
                             [[D, 128], [128 * D, FC], [1, D]])),
        (), [k("Wo_bf")])
    pool.alloc("ao_fm", FC * N)
    ao = pool.ap("ao_fm")
    for ch in range(FC):
        fm_chunk(c, c.tpose, ch, ao[:, ch * N:(ch + 1) * N], "ao", k("ao_fm", ch))
    for t in range(TT):
        def eva(gi, no, nn, pb, pkey, t=t):
            P.v(lambda e, pb=pb, t=t, no=no, nn=nn: e.tensor_tensor(
                c.x_tm[:, t * D + no:t * D + no + nn],
                c.x_tm[:, t * D + no:t * D + no + nn], pb, op=ALU.add),
                [pkey, k("x_tm")], [k("x_tm")])
        proj_tm(c, t, lambda kc, t2: ao[:, kc * N + t2 * 128:kc * N + t2 * 128 + 128],
                lambda kc: k("ao_fm", kc), FC,
                lambda kc, no, nn: Wob[:, kc * D + no:kc * D + no + nn],
                lambda kc: k("Wo_bf"), c.pS0, "pS0", [(0, 512), (512, 512)], eva)
    pool.free("ao_fm", "Wo_bf")


def emit_l1_ffn(c):
    P, pool, w, C = c.P, c.pool, c.w, c.cols
    gr = load_rep(c, "l1_g_rep", D)
    br = load_rep(c, "l1_b_rep", D)
    pool.alloc("h_tm", TT * D)
    htm = pool.ap("h_tm")
    pool.alloc("Wr_f", 64, F32)
    wr = pool.ap("Wr_f", F32)
    P.emit("scalar_q", lambda e: e.dma_start(
        out=wr, in_=bass.AP(w["l1_Wr"].tensor, 0, [[8, 128], [128 * 8, 8], [1, 8]])),
        (), [k("Wr_f")])
    pool.alloc("hT", 1024, F32)
    hT = pool.ap("hT", F32)
    for t in range(TT):
        xt = c.x_tm[:, t * D:(t + 1) * D]
        sc, sck, nb, nbk = ln_stats(c, xt, [k("x_tm")], f"m{t % 2}", True)
        P.s(lambda e, xt=xt, sc=sc, nb=nb: e.activation(
            c.f32a[:, :], xt, AF.Identity, scale=sc, bias=nb),
            [k("x_tm"), sck, nbk], [k("f32a")])
        P.v(lambda e: e.tensor_tensor(c.f32a[:, :], c.f32a[:, :], gr,
                                      op=ALU.mult),
            [k("f32a"), k("rep", "l1_g_rep")], [k("f32a")])
        P.v(lambda e: e.tensor_tensor(c.f32a[:, :], c.f32a[:, :], br,
                                      op=ALU.add),
            [k("f32a"), k("rep", "l1_b_rep")], [k("f32a")])
        P.v(lambda e, t=t: e.tensor_copy(htm[:, t * D:(t + 1) * D],
                                         c.f32a[:, :]),
            [k("f32a")], [k("h_tm", t)])
        stage_tile(c, htm[:, t * D:(t + 1) * D], t, k("h_tm", t), "h1")
        for kc in range(FC):
            P.t(lambda e, kc=kc: e.transpose(
                c.pW[:, 0:128], c.f32a[:, kc * 128:kc * 128 + 128],
                c.idnf[:, :]),
                [k("f32a"), k("idnf")], [k("pW")])
            P.v(lambda e, kc=kc: e.tensor_copy(hT[:, kc * 128:kc * 128 + 128],
                                               c.pW[:, 0:128]),
                [k("pW")], [k("hT", kc)])
        pk = k("pO")
        for kc in range(FC):
            P.t(lambda e, kc=kc: e.matmul(
                c.pO[:, 0:8], hT[:, kc * 128:kc * 128 + 128],
                wr[:, kc * 8:kc * 8 + 8],
                start=(kc == 0), stop=(kc == FC - 1)),
                [k("hT", kc), k("Wr_f"), pk], [pk])
        P.v(lambda e, t=t: e.tensor_copy(c.logit[:, t * 8:t * 8 + 8],
                                         c.pO[:, 0:8]),
            [pk], [k("logit", t)])
        P.v(lambda e, t=t: e.memset(c.logit[:, t * 8 + 7:t * 8 + 8], -1e30),
            (), [k("logit", t)])
    pool.free("hT", "Wr_f")

    lg3 = bass.AP(c.logit, 0, [[128, 128], [8, TT], [1, 8]])
    lkeys = [k("logit", t) for t in range(TT)]
    m1 = C.col("m1", 16)
    m2 = C.col("m2", 16)
    P.v(lambda e: e.tensor_reduce(m1, lg3, axis=AX.X, op=ALU.max),
        lkeys, [k("sm", "m1")])
    m1b = bass.AP(c.sm, C.off("m1"), [[768, 128], [1, 16], [0, 8]])
    P.v(lambda e: e.tensor_tensor(c.rtr[:, 0:128], lg3, m1b, op=ALU.is_ge),
        lkeys + [k("sm", "m1")], [k("rtr", "mask1")])

    pw8b = bass.AP(c.smc, 8, [[64, 128], [0, 16], [1, 8]])
    tpw8b = bass.AP(c.smc, 16, [[64, 128], [0, 16], [1, 8]])

    def first_bit(mof, mkey, tag):
        vv = C.col(tag + "_fb", 16)
        m3 = bass.AP(c.rtr, mof, [[512, 128], [8, 16], [1, 8]])
        P.v(lambda e, m3=m3: e.tensor_tensor(c.rtr[:, 384:512], m3, pw8b,
                                             op=ALU.mult),
            [k("rtr", mkey), k("smc", "pw8")], [k("rtr", "scr")])
        P.v(lambda e, vv=vv: e.tensor_reduce(
            vv, bass.AP(c.rtr, 384, [[512, 128], [8, 16], [1, 8]]),
            axis=AX.X, op=ALU.add),
            [k("rtr", "scr")], [k("sm", tag + "fb")])
        vb = bass.AP(c.sm, C.off(tag + "_fb"), [[768, 128], [1, 16], [0, 8]])
        P.v(lambda e, vb=vb: e.tensor_tensor(c.rtr[:, 384:512], tpw8b, vb,
                                             op=ALU.is_gt),
            [k("sm", tag + "fb"), k("smc", "tpw8")], [k("rtr", "scr")])
        P.v(lambda e, m3=m3: e.tensor_tensor(
            c.rtr[:, mof:mof + 128], m3,
            bass.AP(c.rtr, 384, [[512, 128], [8, 16], [1, 8]]), op=ALU.mult),
            [k("rtr", mkey), k("rtr", "scr")], [k("rtr", mkey)])

    first_bit(0, "mask1", "fb1")
    P.v(lambda e: e.scalar_tensor_tensor(c.rtr[:, 256:384], c.rtr[:, 0:128],
                                         -1e30, lg3, op0=ALU.mult, op1=ALU.add),
        [k("rtr", "mask1")] + lkeys, [k("rtr", "msk")])
    msk3 = bass.AP(c.rtr, 256, [[512, 128], [8, TT], [1, 8]])
    P.v(lambda e: e.tensor_reduce(m2, msk3, axis=AX.X, op=ALU.max),
        [k("rtr", "msk")], [k("sm", "m2")])
    m2b = bass.AP(c.sm, C.off("m2"), [[768, 128], [1, 16], [0, 8]])
    P.v(lambda e: e.tensor_tensor(c.rtr[:, 128:256], msk3, m2b, op=ALU.is_ge),
        [k("rtr", "msk"), k("sm", "m2")], [k("rtr", "mask2")])
    first_bit(128, "mask2", "fb2")
    g1 = C.col("g1", 16)
    g2 = C.col("g2", 16)
    P.v(lambda e: e.tensor_tensor(g1, m1, m2, op=ALU.subtract),
        [k("sm", "m1"), k("sm", "m2")], [k("sm", "g1")])
    P.s(lambda e: e.activation(g2, g1, AF.Sigmoid, scale=c.smc[:, 41:42]),
        [k("sm", "g1"), k("smc", "fneg1")], [k("sm", "g2")])
    P.s(lambda e: e.activation(g1, g1, AF.Sigmoid),
        [k("sm", "g1")], [k("sm", "g1")])
    io3 = bass.AP(c.smc, 0, [[64, 128], [0, 16], [1, 8]])
    idx1 = C.col("idx1", 16)
    idx2 = C.col("idx2", 16)
    for (mof, mkey, ixc, ixkey) in [(0, "mask1", idx1, "idx1"),
                                    (128, "mask2", idx2, "idx2")]:
        P.v(lambda e, mof=mof: e.tensor_tensor(
            c.rtr[:, 384:512], bass.AP(c.rtr, mof, [[512, 128], [8, 16], [1, 8]]),
            io3, op=ALU.mult),
            [k("rtr", mkey), k("smc", "iota8")], [k("rtr", "scr")])
        P.v(lambda e, ixc=ixc: e.tensor_reduce(
            ixc, bass.AP(c.rtr, 384, [[512, 128], [8, 16], [1, 8]]),
            axis=AX.X, op=ALU.add),
            [k("rtr", "scr")], [k("sm", ixkey)])

    pool.alloc("If", N, F32)
    pool.alloc("incl", N, F32)
    pool.alloc("ones14", N, F32)
    pool.alloc("exclT", TT * 48, F32)
    If = pool.ap("If", F32)
    incl = pool.ap("incl", F32)
    ones14 = pool.ap("ones14", F32)
    exT = pool.ap("exclT", F32)
    P.v(lambda e: e.memset(If[0:64, :], 0.0), (), [k("If", "zero")])
    P.v(lambda e: e.memset(ones14[0:64, :], 1.0), (), [k("ones14")])
    for t in range(TT):
        for (ro, nm, mof) in [(0, "mask1", 0), (32, "mask2", 128)]:
            src2 = c.rtr[:, mof + t * 8:mof + t * 8 + 8]
            P.t(lambda e, src2=src2: e.transpose(c.pW[0:8, 0:128], src2,
                                                 c.idnf[:, :]),
                [k("rtr", nm), k("idnf")], [k("pW")])
            P.v(lambda e, ro=ro, t=t: e.tensor_copy(
                If[ro:ro + 7, t * 128:t * 128 + 128], c.pW[0:7, 0:128]),
                [k("pW"), k("If", "zero")], [k("If", (ro, t))])
    ifkeys = [k("If", (ro, t)) for ro in (0, 32) for t in range(TT)]
    P.v(lambda e: e.tensor_tensor_scan(incl[0:64, :], ones14[0:64, :],
                                       If[0:64, :], 0.0,
                                       op0=ALU.mult, op1=ALU.add),
        ifkeys + [k("ones14"), k("If", "zero")], [k("incl")])
    P.emit("gpsimd_q", lambda e: e.dma_start(
        out=bass.AP(c.cnt_in, 0, [[1, 64], [1, 1]]),
        in_=incl[0:64, N - 1:N]),
        [k("incl")], [k("cnt_in")])
    P.emit("cc", lambda e: e.collective_compute(
        "AllGather", ALU.bypass, replica_groups=[list(range(N_CORES))],
        ins=[c.cnt_in.ap().opt()], outs=[c.cnt_out.ap().opt()]),
        [k("cnt_in")], [k("cnt_out")])
    P.emit("gpsimd_q", lambda e: e.dma_start(
        out=c.smc[:, 48:52],
        in_=bass.AP(c.cnt_out, 0, [[1, 128], [128, 4], [1, 1]]),
        allow_slow_non_contiguous=True),
        [k("cnt_out")], [k("smc", "agc")])
    for col in range(4):
        P.t(lambda e, col=col: e.matmul(
            c.pW[0:16, 0:1], c.bmat[:, col * 16:col * 16 + 16],
            c.smc[:, 48 + col:49 + col],
            start=(col == 0), stop=(col == 3)),
            [k("bmat"), k("smc", "agc"), k("pW")], [k("pW")])
    P.v(lambda e: e.tensor_copy(c.smc[0:16, 46:47], c.pW[0:16, 0:1]),
        [k("pW")], [k("smc", "base")])
    P.v(lambda e: e.tensor_tensor(If[0:64, :], incl[0:64, :], If[0:64, :],
                                  op=ALU.subtract),
        [k("incl")] + ifkeys + [k("If", "zero")], [k("excl")])
    for t in range(TT):
        P.t(lambda e, t=t: e.transpose(c.pW[:, 0:40],
                                       If[0:40, t * 128:t * 128 + 128],
                                       c.idnf[0:40, 0:40]),
            [k("excl"), k("idnf")], [k("pW")])
        P.v(lambda e, t=t: e.tensor_copy(exT[:, t * 48:t * 48 + 40],
                                         c.pW[:, 0:40]),
            [k("pW")], [k("exclT", t)])

    def bc_row(src_ap, src_key, dst_cols, dst_name, row0):
        npart = src_ap.shape[0]
        P.t(lambda e, src_ap=src_ap, npart=npart: e.transpose(
            c.pW[0:1, 0:npart], src_ap, c.idnf[0:npart, 0:npart]),
            [src_key, k("idnf")], [k("pW")])
        P.v(lambda e, dst_cols=dst_cols, row0=row0: e.tensor_copy(
            dst_cols[0:1, :], c.pW[0:1, row0:row0 + 7]),
            [k("pW")], [k("sm", dst_name + "_r")])
        bcast_row(c, dst_cols, k("sm", dst_name + "_r"), dst_cols,
                  k("sm", dst_name), dst_cols.shape[1])

    base1r = C.col("base1r", 7)
    base2r = C.col("base2r", 7)
    bc_row(c.smc[0:16, 46:47], k("smc", "base"), base1r, "base1r", 0)
    bc_row(c.smc[0:16, 46:47], k("smc", "base"), base2r, "base2r", 7)

    posl1 = C.col("posl1", 16)
    posl2 = C.col("posl2", 16)
    for t in range(TT):
        for (mof, mkey, po, pokey, exo) in [
                (0, "mask1", posl1, "posl1", 0), (128, "mask2", posl2, "posl2", 32)]:
            P.v(lambda e, mof=mof, t=t, exo=exo: e.tensor_tensor(
                c.rtr[:, 384:391], c.rtr[:, mof + t * 8:mof + t * 8 + 7],
                exT[:, t * 48 + exo:t * 48 + exo + 7], op=ALU.mult),
                [k("rtr", mkey), k("exclT", t)], [k("rtr", "scr")])
            P.v(lambda e, po=po, t=t: e.tensor_reduce(
                po[:, t:t + 1], c.rtr[:, 384:391], axis=AX.X, op=ALU.add),
                [k("rtr", "scr")], [k("sm", pokey)])

    def sel_cols(mof, mkey, valname, out_cols, outname):
        vb = bass.AP(c.sm, C.off(valname), [[768, 128], [0, 16], [1, 7]])
        P.v(lambda e, mof=mof, vb=vb: e.tensor_tensor(
            c.rtr[:, 384:496], bass.AP(c.rtr, mof, [[512, 128], [8, 16], [1, 7]]),
            vb, op=ALU.mult),
            [k("rtr", mkey), k("sm", valname)], [k("rtr", "scr")])
        P.v(lambda e, out_cols=out_cols: e.tensor_reduce(
            out_cols, bass.AP(c.rtr, 384, [[512, 128], [7, 16], [1, 7]]),
            axis=AX.X, op=ALU.add),
            [k("rtr", "scr")], [k("sm", outname)])

    b1s = C.col("b1s", 16)
    b2s = C.col("b2s", 16)
    sel_cols(0, "mask1", "base1r", b1s, "b1s")
    sel_cols(128, "mask2", "base2r", b2s, "b2s")
    kp1 = C.col("kp1", 16)
    kp2 = C.col("kp2", 16)
    pg1 = C.col("pg1", 16)
    pg2 = C.col("pg2", 16)
    P.v(lambda e: e.tensor_tensor(pg1, posl1, b1s, op=ALU.add),
        [k("sm", "posl1"), k("sm", "b1s")], [k("sm", "pg1")])
    P.v(lambda e: e.tensor_scalar(kp1, pg1, float(CAPACITY), None, op0=ALU.is_lt),
        [k("sm", "pg1")], [k("sm", "kp1")])
    P.v(lambda e: e.tensor_tensor(pg2, posl2, b2s, op=ALU.add),
        [k("sm", "posl2"), k("sm", "b2s")], [k("sm", "pg2")])
    P.v(lambda e: e.tensor_scalar(kp2, pg2, float(CAPACITY), None, op0=ALU.is_lt),
        [k("sm", "pg2")], [k("sm", "kp2")])
    sl1 = C.col("sl1", 16)
    sl2 = C.col("sl2", 16)
    P.v(lambda e: e.scalar_tensor_tensor(sl1, idx1, float(SHARD), pg1,
                                         op0=ALU.mult, op1=ALU.add),
        [k("sm", "idx1"), k("sm", "pg1")], [k("sm", "sl1")])
    P.v(lambda e: e.scalar_tensor_tensor(sl2, idx2, float(SHARD), pg2,
                                         op0=ALU.mult, op1=ALU.add),
        [k("sm", "idx2"), k("sm", "pg2")], [k("sm", "sl2")])
    tcol = c.smc[:, 24:25]
    for nm, sl, kpn in [("sl1", sl1, "kp1"), ("sl2", sl2, "kp2")]:
        kpc = kp1 if kpn == "kp1" else kp2
        P.v(lambda e, sl=sl: e.tensor_scalar(sl, sl, tcol, None,
                                             op0=ALU.subtract),
            [k("sm", nm), k("smc", "trash_iota")], [k("sm", nm)])
        P.v(lambda e, sl=sl, kpc=kpc, kpn=kpn: e.tensor_tensor(
            sl, sl, kpc, op=ALU.mult),
            [k("sm", nm), k("sm", kpn)], [k("sm", nm)])
        P.v(lambda e, sl=sl: e.tensor_scalar(sl, sl, tcol, None, op0=ALU.add),
            [k("sm", nm), k("smc", "trash_iota")], [k("sm", nm)])
    gk1 = C.col("gk1", 16)
    gk2 = C.col("gk2", 16)
    P.v(lambda e: e.tensor_tensor(gk1, g1, kp1, op=ALU.mult),
        [k("sm", "g1"), k("sm", "kp1")], [k("sm", "gk1")])
    P.v(lambda e: e.tensor_tensor(gk2, g2, kp2, op=ALU.mult),
        [k("sm", "g2"), k("sm", "kp2")], [k("sm", "gk2")])
    P.v(lambda e: e.tensor_copy(c.idx32[:, 0:16], sl1), [k("sm", "sl1")],
        [k("idx32", 1)])
    P.v(lambda e: e.tensor_copy(c.idx32[:, 16:32], sl2), [k("sm", "sl2")],
        [k("idx32", 2)])
    if DEBUG_TAP == "router":
        P.emit("sync_q", lambda e: e.dma_start(out=bass.AP(c.dbgf, 0, [[1024, 128], [1, 300]]),
                                               in_=c.sm[:, 300:600]),
               [k("sm", None)], [k("dbgf")])
        P.emit("sync_q", lambda e: e.dma_start(out=c.dbgu.ap(), in_=c.idx32[:, :]),
               [k("idx32", None)], [k("dbgu")])
        return
    pool.free("If", "incl", "ones14", "exclT")

    for r, ikey in [(0, 1), (1, 2)]:
        for t in range(TT):
            P.emit("gpsimd_q", lambda e, r=r, t=t: e.indirect_dma_start(
                c.mslots.ap(), IOA(ap=c.idx32[:, r * 16 + t:r * 16 + t + 1],
                                   axis=0),
                htm[:, t * D:(t + 1) * D], None),
                [k("idx32", ikey), k("h_tm", t), k("mslots", None)],
                [k("mslots", "scat")])

    P.emit("sync_q", lambda e: e.dma_start(
        out=bass.AP(c.xspill, 0, [[1024, 128], [128 * 1024, TT], [1, 1024]]),
        in_=ap3(c.x_tm, 0, TT * D, [(1024, TT), (1, 1024)])),
        [k("x_tm")], [k("xspill")])
    pool.free("h_tm")

    P.emit("cc", lambda e: e.collective_compute(
        "ReduceScatter", ALU.add, replica_groups=[list(range(N_CORES))],
        ins=[c.mslots.ap().opt()], outs=[c.msred.ap().opt()]),
        [k("mslots", None)], [k("msred")])

    pool.alloc("hg", FC * 768)
    pool.alloc("Hbuf", 32 * 768)
    pool.alloc("w13a", 2 * 2048)
    pool.alloc("w13b", 2 * 2048)
    pool.alloc("w2q", 32 * 256)
    pool.alloc("sout", 6 * 1024)
    hg = pool.ap("hg")
    Hb = pool.ap("Hbuf")
    w13 = {0: pool.ap("w13a"), 1: pool.ap("w13b")}
    w2q = pool.ap("w2q")
    sout = pool.ap("sout")

    def ffn_unit(cols, in_ap, in_keys, w1src, w3src, w2src, out_cb):
        nfc = DFF // 256
        lngs = [(0, 512), (512, cols - 512)] if cols > 512 else [(0, cols)]
        for fc in range(nfc):
            par = fc % 2
            buf = w13[par]
            bkey = "w13a" if par == 0 else "w13b"

            def w13_load(e, ws, dsl, fc=fc, buf=buf):
                return e.dma_start(out=buf[:, dsl:dsl + 2048], in_=bass.AP(
                    ws.tensor, fc * 256,
                    [[DFF, 128], [128 * DFF, FC], [1, 256]]))
            P.emit("gpsimd_q",
                   lambda e, w13_load=w13_load: w13_load(e, w1src, 0),
                   [k(bkey)], [k(bkey, "w1")])
            P.emit("gpsimd_q",
                   lambda e, w13_load=w13_load: w13_load(e, w3src, 2048),
                   [k(bkey)], [k(bkey, "w3")])
            for (no, nn) in lngs:
                for m2 in range(2):
                    pa = c.pS0[:, 0:nn]
                    pb_ = c.pS1[:, 0:nn]
                    for kc in range(FC):
                        P.t(lambda e, fc=fc, m2=m2, kc=kc, no=no, nn=nn,
                            buf=buf, pa=pa: e.matmul(
                                pa, buf[:, kc * 256 + m2 * 128:
                                        kc * 256 + m2 * 128 + 128],
                                in_ap(kc, no, nn),
                                start=(kc == 0), stop=(kc == FC - 1)),
                            [k(bkey, "w1")] + in_keys + [k("pS0")], [k("pS0")])
                    P.s(lambda e, nn=nn, pa=pa: e.activation(
                        c.eB[:, 0:nn], pa, AF.Silu),
                        [k("pS0")], [k("eB")])
                    for kc in range(FC):
                        P.t(lambda e, fc=fc, m2=m2, kc=kc, no=no, nn=nn,
                            buf=buf, pb_=pb_: e.matmul(
                                pb_, buf[:, 2048 + kc * 256 + m2 * 128:
                                         2048 + kc * 256 + m2 * 128 + 128],
                                in_ap(kc, no, nn),
                                start=(kc == 0), stop=(kc == FC - 1)),
                            [k(bkey, "w3")] + in_keys + [k("pS1")], [k("pS1")])
                    hrow = fc * 2 + m2
                    P.v(lambda e, hrow=hrow, no=no, nn=nn, pb_=pb_, cols=cols:
                        e.tensor_tensor(
                            Hb[:, hrow * cols + no:hrow * cols + no + nn],
                            pb_, c.eB[:, 0:nn], op=ALU.mult),
                        [k("pS1"), k("eB")], [k("Hbuf", hrow)])
        nsub = (cols + 127) // 128
        for q4 in range(4):
            def w2_load(e, q4=q4, ws=w2src):
                return e.dma_start(out=w2q, in_=bass.AP(
                    ws.tensor, q4 * 256, [[D, 128], [128 * D, 32], [1, 256]]))
            P.emit("gpsimd_q", w2_load, [k("w2q")], [k("w2q", "d")])
            for s in range(nsub):
                ssz = min(128, cols - 128 * s)
                po = c.pO[:, 0:256]
                for kc in range(32):
                    P.t(lambda e, s=s, kc=kc, ssz=ssz, cols=cols, po=po:
                        e.matmul(
                            po[0:ssz, :], Hb[:, kc * cols + s * 128:
                                             kc * cols + s * 128 + ssz],
                            w2q[:, kc * 256:kc * 256 + 256],
                            start=(kc == 0), stop=(kc == 31)),
                        [k("Hbuf", kc), k("w2q", "d"), k("pO")], [k("pO")])
                out_cb(q4, s, ssz, po)

    groups = []
    o = 0
    while o < SHARD:
        gsz = min(768, SHARD - o)
        groups.append((o, gsz))
        o += gsz
    for (go, gsz) in groups:
        for ch in range(FC):
            src2 = bass.AP(c.msred, go * D + ch * 128, [[D, gsz], [1, 128]])
            P.emit("scalar_q", lambda e, ch=ch, src2=src2, gsz=gsz: e.dma_start(
                out=hg[:, ch * 768:ch * 768 + gsz], in_=src2, transpose=True),
                [k("msred"), k("xbar"), k("hg", None)], [k("hg", ch), k("xbar")])

        def rout_out(q4, s, ssz, po, go=go):
            P.v(lambda e, q4=q4, s=s, ssz=ssz, po=po: e.tensor_copy(
                sout[0:ssz, s * 1024 + q4 * 256:s * 1024 + q4 * 256 + 256],
                po[0:ssz, :]),
                [k("pO")], [k("sout", (s, q4))])
            if q4 == 3:
                row0 = go + s * 128
                P.emit("sync_q", lambda e, row0=row0, ssz=ssz: e.dma_start(
                    out=bass.AP(c.msoutl, row0 * D, [[D, ssz], [1, D]]),
                    in_=sout[0:ssz, s * 1024:s * 1024 + 1024]),
                    [k("sout", (s, qq)) for qq in range(4)],
                    [k("msoutl")])

        ffn_unit(gsz,
                 lambda kc, no, nn: hg[:, kc * 768 + no:kc * 768 + no + nn],
                 [k("hg", ch2) for ch2 in range(FC)],
                 w["myW1_in"], w["myW3_in"], w["myW2_in"], rout_out)

    P.emit("cc", lambda e: e.collective_compute(
        "AllGather", ALU.bypass, replica_groups=[list(range(N_CORES))],
        ins=[c.msoutl.ap().opt()], outs=[c.mslotout.ap().opt()]),
        [k("msoutl")], [k("mslotout", "all")])

    P.emit("sync_q", lambda e: e.dma_start(
        out=ap3(c.x_tm, 0, TT * D, [(1024, TT), (1, 1024)]),
        in_=bass.AP(c.xspill, 0, [[1024, 128], [128 * 1024, TT], [1, 1024]])),
        [k("xspill"), k("x_tm")], [k("x_tm")])
    pool.free("hg")
    pool.alloc("hs", FC * 512)
    hs = pool.ap("hs")
    for tg in range(4):
        for ch in range(FC):
            src2 = bass.AP(c.tpose, tg * 512 * D + ch * 128, [[D, 512], [1, 128]])
            P.emit("scalar_q", lambda e, ch=ch, src2=src2: e.dma_start(
                out=hs[:, ch * 512:(ch + 1) * 512], in_=src2, transpose=True),
                [k(c.tpose.name), k("xbar"), k("hs", None)],
                [k("hs", ch), k("xbar")])

        def shared_out(q4, s, ssz, po, tg=tg):
            t = tg * 4 + s
            P.v(lambda e, q4=q4, t=t, po=po: e.tensor_tensor(
                c.x_tm[:, t * D + q4 * 256:t * D + q4 * 256 + 256],
                c.x_tm[:, t * D + q4 * 256:t * D + q4 * 256 + 256],
                po[0:128, :], op=ALU.add),
                [k("pO"), k("x_tm")], [k("x_tm")])

        ffn_unit(512,
                 lambda kc, no, nn: hs[:, kc * 512 + no:kc * 512 + no + nn],
                 [k("hs", ch2) for ch2 in range(FC)],
                 c.w["l1_W1s"], c.w["l1_W3s"], c.w["l1_W2s"], shared_out)

    pool.free("Hbuf", "w13a", "w13b", "w2q", "sout", "hs")
    pool.alloc("gb", 1024)
    gb = pool.ap("gb")
    for r, ikey, gcol, gname in [(0, 1, gk1, "gk1"), (1, 2, gk2, "gk2")]:
        for t in range(TT):
            P.emit("gpsimd_q", lambda e, r=r, t=t: e.indirect_dma_start(
                gb, None, c.mslotout.ap(),
                IOA(ap=c.idx32[:, r * 16 + t:r * 16 + t + 1], axis=0)),
                [k("mslotout", "all"), k("idx32", ikey), k("gb")],
                [k("gb")]) if False else None
            P.emit("gpsimd_q", lambda e, r=r, t=t: e.indirect_dma_start(
                gb, None, c.mslotout.ap(),
                IOA(ap=c.idx32[:, r * 16 + t:r * 16 + t + 1], axis=0)),
                [k("mslotout", "all"), k("idx32", ikey)],
                [k("gb")])
            P.v(lambda e, t=t, gcol=gcol: e.scalar_tensor_tensor(
                c.x_tm[:, t * D:(t + 1) * D], gb, gcol[:, t:t + 1],
                c.x_tm[:, t * D:(t + 1) * D], op0=ALU.mult, op1=ALU.add),
                [k("gb"), k("sm", gname), k("x_tm")], [k("x_tm")])
    pool.free("gb")


def host_constants(core_id):
    cc = {}
    cc["idn_bf"] = np.eye(128, dtype=ml_dtypes.bfloat16)
    cc["idn_f32"] = np.eye(128, dtype=np.float32)
    m = np.zeros((128, 4, 512), np.float32)
    for ri, r in enumerate((0, 128, 256, 384)):
        s = np.arange(128)[:, None]
        q = np.arange(512)[None, :]
        m[:, ri, :] = np.where(q >= r + s, 0.0, -1e30)
    cc["tril"] = np.ascontiguousarray(
        m.reshape(128, 4 * 512).astype(ml_dtypes.bfloat16))
    t = np.arange(N).reshape(TT, 128).T
    j = np.arange(32, dtype=np.float64)
    inv = 1.0 / (10000.0 ** (j / 32.0))
    ang = t[..., None].astype(np.float64) * inv[None, None, :]
    cos = np.cos(ang).astype(np.float32)
    sin = np.sin(ang).astype(np.float32)
    cc["rope_cos"] = np.ascontiguousarray(
        np.concatenate([cos, cos], axis=2).reshape(128, TT * 64))
    cc["rope_sin"] = np.ascontiguousarray(
        np.concatenate([-sin, sin], axis=2).reshape(128, TT * 64))
    io = np.zeros((128, 8), np.float32)
    io[:, :7] = np.arange(7, dtype=np.float32)[None, :]
    cc["iota8"] = io
    pw = np.zeros((128, 8), np.float32)
    pw[:, :7] = 2.0 ** (6 - np.arange(7, dtype=np.float32))[None, :]
    cc["pw8"] = pw
    cc["tpw8"] = np.ascontiguousarray(pw * 2.0)
    M = np.zeros((128, 4, 16), np.float32)
    for co in range(N_CORES):
        for e in range(E_ROUTED):
            r0 = 64 * co + e
            r1 = 64 * co + 32 + e
            if co < core_id:
                M[r0 % 128, r0 // 128, e] = 1.0
                M[r1 % 128, r1 // 128, 7 + e] = 1.0
            M[r0 % 128, r0 // 128, 7 + e] = 1.0
    cc["basemat"] = np.ascontiguousarray(M.reshape(128, 64))
    cc["trash_iota"] = (TRASH + np.arange(128, dtype=np.float32)).reshape(128, 1)
    cc["onescol"] = np.ones((128, 1), np.float32)
    cc["onesrow"] = np.ones((1, 128), np.float32)
    return cc


def prep_inputs(inputs):
    g = {}
    f32 = np.float32

    def T(a):
        return np.ascontiguousarray(np.asarray(a, f32).T)

    g["l0_wqT"] = T(inputs["L0_attn_wq"])
    g["l0_wkT"] = T(inputs["L0_attn_wk"])
    g["l0_wvT"] = T(inputs["L0_attn_wv"])
    g["l0_woT"] = T(inputs["L0_attn_wo"])
    g["l0_W_T"] = T(inputs["L0_ffn_W"])
    g["l0_WpT"] = T(inputs["L0_ffn_Wp"])
    wd = np.concatenate([np.asarray(inputs[f"L0_ffn_Wd{i}"], f32)
                         for i in range(4)], axis=0)
    wdT = np.zeros((D, 16), f32)
    wdT[:, :15] = wd.T
    g["l0_WdT"] = wdT
    wu = np.concatenate([np.asarray(inputs[f"L0_ffn_Wu{i}"], f32)
                         for i in range(4)], axis=1)
    wuT = np.zeros((16, D), f32)
    wuT[:15, :] = wu.T
    g["l0_WuT"] = wuT
    for nm, src in [("l1_Wq", "L1_attn_W_q"), ("l1_Wdkv", "L1_attn_W_dkv"),
                    ("l1_Wuk", "L1_attn_W_uk"), ("l1_Wuv", "L1_attn_W_uv"),
                    ("l1_Wo", "L1_attn_W_o"),
                    ("l1_W1s", "L1_ffn_W1s"), ("l1_W3s", "L1_ffn_W3s"),
                    ("l1_W2s", "L1_ffn_W2s")]:
        g[nm] = np.ascontiguousarray(np.asarray(inputs[src], f32))
    wr = np.zeros((D, 8), f32)
    wr[:, :7] = np.asarray(inputs["L1_ffn_Wr"], f32)
    g["l1_Wr"] = wr

    def pp(vec, chunks):
        return np.ascontiguousarray(np.asarray(vec, f32).reshape(chunks, 128).T)

    def rep(vec):
        v = np.asarray(vec, f32)
        return np.ascontiguousarray(np.broadcast_to(v, (128, v.shape[0])))

    g["l0_bq_rep"] = rep(inputs["L0_attn_bq"])
    g["l0_bk_rep"] = rep(inputs["L0_attn_bk"])
    g["l0_bv_rep"] = rep(inputs["L0_attn_bv"])
    g["l0_bo_rep"] = rep(inputs["L0_attn_bo"])
    g["l0_g_rep"] = rep(inputs["L0_attn_ln_g"])
    g["l0_b_rep"] = rep(inputs["L0_attn_ln_b"])
    g["l0_bp_rep"] = rep(inputs["L0_ffn_bp"])
    sim = np.asarray(inputs["L0_ffn_sim"], f32)
    g["l0_sim"] = np.ascontiguousarray(
        np.concatenate([sim[:128], sim[128:]], axis=1))
    g["l0_temp"] = np.asarray(inputs["L0_ffn_temp"], f32).reshape(1, 1)
    g["l1_gamma_pp"] = pp(inputs["L1_attn_ln_gamma"], 2)
    g["l1_beta_pp"] = pp(inputs["L1_attn_ln_beta"], 2)
    g["l1_g_rep"] = rep(inputs["L1_ffn_ln_g"])
    g["l1_b_rep"] = rep(inputs["L1_ffn_ln_b"])
    return g


def kernel(**inputs):
    glob = prep_inputs(inputs)
    x_full = np.asarray(inputs["L0_attn_x"], np.float32)
    ew = {nm: np.asarray(inputs[src], np.float32)
          for nm, src in [("myW1_in", "L1_ffn_W1"), ("myW3_in", "L1_ffn_W3"),
                          ("myW2_in", "L1_ffn_W2")]}
    nc = build_program()
    in_maps = []
    for cc in range(N_CORES):
        m = dict(glob)
        m["x"] = np.ascontiguousarray(x_full[cc])
        m.update(host_constants(cc))
        for nm in ("myW1_in", "myW3_in", "myW2_in"):
            m[nm] = np.ascontiguousarray(ew[nm][min(cc, 6)])
        in_maps.append(m)
    res = run_bass_kernel_spmd(nc, in_maps, core_ids=list(range(N_CORES)))
    out = np.empty((N_CORES, N, D), np.float32)
    for cc in range(N_CORES):
        out[cc] = res.results[cc]["y"]
    return out
